# version 24
# baseline (speedup 1.0000x reference)
import math
import numpy as np
import concourse.bass as bass
import concourse.mybir as mybir
from concourse.bass_utils import run_bass_kernel_spmd

F32 = mybir.dt.float32
F16 = mybir.dt.float16
BF16 = mybir.dt.bfloat16
I32 = mybir.dt.int32
I8 = mybir.dt.int8
AF = mybir.ActivationFunctionType
ALU = mybir.AluOpType
AXX = mybir.AxisListType.X

S = 2048
D = 1024
H = 16
DH = 64
DHR = 32
DL = 256
DQK = DH + DHR
FF = 4096
RL = 16
ISQ = 1.0 / math.sqrt(float(DQK))
TC = 256
NTC = S // TC


class Prog:

    def __init__(self, nc):
        self.nc = nc
        self.ops = {"sync": [], "vector": [], "scalar": [], "tensor": [], "gpsimd": []}
        self.ch = {e: nc.alloc_semaphore("ch_" + e)
                   for e in ("vector", "scalar", "gpsimd", "tensor")}
        self.cnt = {"vector": 0, "scalar": 0, "gpsimd": 0, "tensor": 0}
        self.dsem = {}
        self.dexp = {}

    @staticmethod
    def _waits(e, waits):
        best = {}
        for sem, val in waits:
            k = id(sem)
            if k not in best or best[k][1] < val:
                best[k] = (sem, val)
        for sem, val in best.values():
            e.wait_ge(sem, val)

    def _chained(self, eng, fn, waits):
        ch = self.ch[eng]
        w = tuple((s, v) for (s, v) in ((p for p in waits if p is not None))
                  if s is not None and s is not ch)
        n0 = self.cnt[eng]

        def emit(e, fn=fn, w=w, ch=ch, n0=n0):
            Prog._waits(e, w)
            if n0:
                e.wait_ge(ch, n0)
            fn(e).then_inc(ch, 1)
        self.ops[eng].append(emit)
        self.cnt[eng] += 1
        return (ch, self.cnt[eng])

    def v(self, fn, waits=()):
        return self._chained("vector", fn, waits)

    def s(self, fn, waits=()):
        return self._chained("scalar", fn, waits)

    def g(self, fn, waits=()):
        return self._chained("gpsimd", fn, waits)

    def t(self, fn, waits=(), inc=False):
        ch = self.ch["tensor"]
        w = tuple((s, v) for (s, v) in (p for p in waits if p is not None)
                  if s is not None and s is not ch)
        if inc:
            def emit(e, fn=fn, w=w, ch=ch):
                Prog._waits(e, w)
                fn(e).then_inc(ch, 1)
            self.ops["tensor"].append(emit)
            self.cnt["tensor"] += 1
            return (ch, self.cnt["tensor"])

        def emit2(e, fn=fn, w=w):
            Prog._waits(e, w)
            fn(e)
        self.ops["tensor"].append(emit2)
        return None

    NSLOT = 4

    def d(self, queue, semname, fn, waits=()):
        if semname not in self.dsem:
            self.dsem[semname] = [self.nc.alloc_semaphore("d_%s%d" % (semname, i))
                                  for i in range(self.NSLOT)]
            self.dexp[semname] = 0
        i = self.dexp[semname]
        self.dexp[semname] += 1
        sem = self.dsem[semname][i % self.NSLOT]
        tgt = 16 * (i // self.NSLOT + 1)
        w = [(s, v) for (s, v) in (p for p in waits if p is not None)
             if s is not None]
        if tgt > 16:
            w.append((sem, tgt - 16))
        w = tuple(w)

        def emit(e, fn=fn, w=w, sem=sem):
            Prog._waits(e, w)
            fn(e).then_inc(sem, 16)
        self.ops[queue].append(emit)
        return (sem, tgt)

    def stream_now(self, semname):
        if semname not in self.dsem:
            return []
        n = self.dexp[semname]
        toks = []
        for sl in range(self.NSLOT):
            cnt = (n - sl + self.NSLOT - 1) // self.NSLOT
            if cnt > 0:
                toks.append((self.dsem[semname][sl], 16 * cnt))
        return toks

    def raw(self, queue, fn):
        self.ops[queue].append(fn)

    def replay(self):
        with self.nc.Block("main") as blk:
            blk.sync(lambda e: [f(e) for f in self.ops["sync"]])
            blk.vector(lambda e: [f(e) for f in self.ops["vector"]])
            blk.scalar(lambda e: [f(e) for f in self.ops["scalar"]])
            blk.tensor(lambda e: [f(e) for f in self.ops["tensor"]])
            blk.gpsimd(lambda e: [f(e) for f in self.ops["gpsimd"]])


class SbAlloc:
    def __init__(self, nc):
        self.nc = nc
        self.BASE = (nc.sbuf_base + 63) // 64 * 64
        self.LIMIT = nc.sbuf_top
        self.placed = []

    def alloc(self, name, shape, dtype, ph0, ph1, align=64, side=None):
        nb = int(np.prod(shape[1:])) * mybir.dt.size(dtype)
        high = (ph1 >= 5 and ph0 <= 4) or (ph1 - ph0) >= 3
        if side == "high":
            high = True
        if high:
            cands = sorted({self.LIMIT - nb} | {s - nb for (s, e, a, b, n)
                                                in self.placed}, reverse=True)
        else:
            cands = sorted({self.BASE} | {e for (s, e, a, b, n) in self.placed})
        for off in cands:
            off = (off // align * align) if high else ((off + align - 1)
                                                       // align * align)
            if off < self.BASE or off + nb > self.LIMIT:
                continue
            if all((ph1 < a or b < ph0) or (off + nb <= s or e <= off)
                   for (s, e, a, b, n) in self.placed):
                self.placed.append((off, off + nb, ph0, ph1, name))
                return self.nc.alloc_sbuf_tensor_at(name, list(shape), dtype, offset=off)
        raise RuntimeError("SBUF full for %s (%d B)" % (name, nb))


class Builder:
    def __init__(self):
        nc = bass.Bass("TRN2", target_bir_lowering=False)
        self.nc = nc
        self.P = Prog(nc)
        self.dd = {}

        def din(name, shape):
            self.dd[name] = nc.dram_tensor(name, list(shape), F32, kind="ExternalInput")
            return self.dd[name]
        self.xT = din("xT", [D, S])
        self.wDQ = din("wDQ", [D, DL]); self.wUQ = din("wUQ", [DL, D])
        self.wQR = din("wQR", [DL, H * DHR]); self.wDKV = din("wDKV", [D, DL])
        self.wUK = din("wUK", [DL, D]); self.wUV = din("wUV", [DL, D])
        self.wKR = din("wKR", [D, DHR])
        self.wOr = din("wOr", [D, D])
        self.wFCr = din("wFCr", [64 * 128, 512])
        self.wPRr = din("wPRr", [64 * 128, 512])
        self.bFC = din("bFC", [128, FF // 128])
        self.aFCT = din("aFCT", [D, RL]); self.bFCbT = din("bFCbT", [RL, FF])
        self.bPR = din("bPR", [128, D // 128])
        self.aPRT = din("aPRT", [FF, RL]); self.bPRbT = din("bPRbT", [RL, D])
        self.ropeC = din("ropeC", [128, S]); self.ropeD = din("ropeD", [128, S])
        self.pswap = din("pswap", [128, 128]); self.trim = din("trim", [128, 1024])
        self.ident = din("ident", [128, 128]); self.sel2 = din("sel2", [16, 1024])
        self.out = nc.dram_tensor("out", [D, S], F32, kind="ExternalOutput")

        A = SbAlloc(nc)
        self.XA = A.alloc("XA", [128, 8 * S], BF16, 0, 3)
        self.CQ = A.alloc("CQ", [128, 2 * S], BF16, 0, 2)
        self.CKV = A.alloc("CKV", [128, 2 * S], BF16, 0, 2)
        self.QRt = A.alloc("QRt", [128, 4 * S], BF16, 0, 2)
        self.KRt = A.alloc("KRt", [32, S], BF16, 0, 2)
        self.WUQ = A.alloc("WUQ", [128, 2 * D], BF16, 0, 2)
        self.WUK = A.alloc("WUK", [128, 2 * D], BF16, 0, 2)
        self.WQR = A.alloc("WQR", [128, 2 * H * DHR], BF16, 0, 0)
        self.WUV = A.alloc("WUV", [128, 2 * D], BF16, 0, 1)
        self.WDQ = A.alloc("WDQ", [128, 8 * DL], BF16, 0, 0)
        self.WDKV = A.alloc("WDKV", [128, 8 * DL], BF16, 0, 0)
        self.WKR = A.alloc("WKR", [128, 8 * DHR], BF16, 0, 0)
        self.RC = A.alloc("RC", [128, S], BF16, 0, 0)
        self.RDm = A.alloc("RDm", [128, S], BF16, 0, 0)
        self.PSW = A.alloc("PSW", [128, 128], BF16, 0, 0)
        self.STG = A.alloc("STG", [128, 4096], F32, 0, 2)
        self.SC1 = A.alloc("SC1", [128, S], BF16, 0, 0)
        self.SC2 = A.alloc("SC2", [128, S], BF16, 0, 0)
        self.VP = A.alloc("VP", [128, 16 * 1040], BF16, 1, 2)
        self.ODSC = A.alloc("ODSC", [64, S], BF16, 2, 2)
        self.QHB = [A.alloc("QH%d" % i, [96, S], BF16, 1, 2) for i in range(2)]
        self.KHB = [A.alloc("KH%d" % i, [96, S], BF16, 1, 2) for i in range(2)]
        self.CLB = [A.alloc("CL%d" % i, [128, 1024], F16, 2, 2) for i in range(4)]
        self.PTB = [A.alloc("PT%d" % i, [128, 1024], BF16, 2, 2) for i in range(4)]
        self.TRI = A.alloc("TRI", [128, 1024], F32, 0, 2)
        self.DRW = A.alloc("DRW", [128, S], F32, 2, 3, side="high")
        self.DNT = A.alloc("DNT", [16, S], F32, 2, 3, side="high")
        self.DNTB = A.alloc("DNTB", [16, S], BF16, 3, 3, side="high")
        self.MXW = A.alloc("MXW", [128, 64], F32, 2, 7)
        self.MXR = A.alloc("MXR", [128, 2], F32, 2, 7)
        self.MX1 = A.alloc("MX1", [1, 64], F32, 2, 7)
        self.SML = A.alloc("SML", [128, 16], F32, 2, 7)
        self.ONEC = A.alloc("ONEC", [128, 8], BF16, 0, 7)
        self.ONERB = A.alloc("ONERB", [1, 128], BF16, 0, 7)
        self.ONERF = A.alloc("ONERF", [1, 128], F32, 0, 7)
        self.IDT = A.alloc("IDT", [128, 128], BF16, 0, 7)
        self.SELB = A.alloc("SELB", [16, 1024], BF16, 0, 3)
        self.AFT = A.alloc("AFT", [128, 128], BF16, 0, 7)
        self.APT = A.alloc("APT", [128, 512], BF16, 0, 7)
        self.XFM = A.alloc("XFM", [128, 8 * S], F32, 3, 4)
        self.WOBs = A.alloc("WOBs", [128, D], BF16, 3, 3)
        self.WST = A.alloc("WST", [128, 2 * D], F32, 3, 3)
        self.RB2 = A.alloc("RB2", [128, S], BF16, 3, 3)
        self.RTF = A.alloc("RTF", [128, 8 * FF], BF16, 3, 7)
        self.QSG = [A.alloc("QSG%d" % i, [128, 512], F32, 3, 5) for i in range(3)]
        self.QTI = [A.alloc("QTI%d" % i, [128, 512], I8, 3, 5) for i in range(2)]
        self.YC1 = A.alloc("YC1", [128, S], BF16, 4, 4)
        self.STY = A.alloc("STY", [1, S], F32, 4, 4)
        self.STY2 = A.alloc("STY2", [1, S], F32, 4, 4)
        self.MBC = A.alloc("MBC", [128, 1024], F32, 4, 4)
        self.RBC = A.alloc("RBC", [128, 1024], F32, 4, 4)
        self.XT16 = A.alloc("XT16", [128, 8 * S], BF16, 4, 7)
        self.RTP = A.alloc("RTP", [128, 32 * D], BF16, 5, 7)
        self.HH = A.alloc("HH", [128, 32 * TC], BF16, 5, 7)
        self.BSF = A.alloc("BSF", [16, FF], BF16, 5, 7)
        self.Y2C = A.alloc("Y2C", [128, 8 * TC], BF16, 5, 7)
        self.UO = A.alloc("UO", [16, S], BF16, 5, 7, side="high")
        self.BSP = A.alloc("BSP", [16, D], BF16, 5, 7, side="high")
        self.UPt = A.alloc("UPt", [16, TC], BF16, 5, 7, side="high")
        self.BFCS = A.alloc("BFCS", [128, 32], F32, 5, 7, side="high")
        self.BPRS = A.alloc("BPRS", [128, 8], F32, 5, 7, side="high")
        self.PTMP = A.alloc("PTMP", [128, 2 * TC], F32, 5, 7)
        self.TSQ = A.alloc("TSQ", [128, TC], BF16, 7, 7)
        self.OST = [A.alloc("OST%d" % i, [128, TC], F32, 7, 7) for i in range(2)]
        self.L2R = [A.alloc("L2R%d" % i, [1, TC], F32, 7, 7) for i in range(3)]
        self.M2R = A.alloc("M2R", [1, TC], BF16, 7, 7, side="high")
        self.M2B = A.alloc("M2B", [128, TC], BF16, 7, 7)
        self.R2B = A.alloc("R2B", [128, TC], F32, 7, 7)

        self.BIG = nc.alloc_psum_tensor("BIG", [128, 1024], F32)
        self.PSD = [nc.alloc_psum_tensor("PSD%d" % i, [128, 1024], F32)
                    for i in range(3)]
        self.PSB = [self.PSD[i][:, 512 * hh:512 * hh + 512]
                    for i in range(3) for hh in range(2)]
        self.ps_free = [None] * 6
        self.stg_free = [None, None]
        self.stage_i = 0
        xoff = next(s for (s, e, a, b, n) in A.placed if n == "XFM")
        okn = {"CQ", "CKV", "QRt", "KRt", "WUQ", "WUK"}
        self.x2_early_n = 0
        for d in range(8):
            lo, hi = xoff + 8192 * d, xoff + 8192 * (d + 1)
            if all((not (s < hi and lo < e)) or b < 2 or a > 2 or n in okn
                   for (s, e, a, b, n) in A.placed):
                self.x2_early_n = d + 1
            else:
                break

    def stage(self, src_ap, width, cast_fn):
        h = self.stage_i % 2
        self.stage_i += 1
        sl = self.STG[:, 2048 * h:2048 * h + width]
        ld = self.P.d("sync", "din",
                      lambda e, sl=sl, src_ap=src_ap: e.dma_start(sl, src_ap),
                      waits=[self.stg_free[h]] if self.stg_free[h] else [])
        tok = cast_fn(sl, ld)
        self.stg_free[h] = tok
        return tok

    def stage3(self, src_ap3, width, a, cast_fn):
        h = self.stage_i % 2
        self.stage_i += 1
        sl = self.STG[:, 2048 * h:2048 * h + width]
        sl3 = sl.rearrange("p (a r) -> p a r", a=a)
        ld = self.P.d("sync", "din",
                      lambda e, sl3=sl3, src_ap3=src_ap3: e.dma_start(sl3, src_ap3),
                      waits=[self.stg_free[h]] if self.stg_free[h] else [])
        tok = cast_fn(sl, ld)
        self.stg_free[h] = tok
        return tok

    def mmg(self, outap, parts, waits, inc=True):
        tok = None
        n = len(parts)
        for i, (la, ra) in enumerate(parts):
            tok = self.P.t(
                lambda e, la=la, ra=ra, outap=outap, st=(i == 0), sp=(i == n - 1):
                e.matmul(outap, la, ra, start=st, stop=sp),
                waits=(waits if i == 0 else ()), inc=(inc and i == n - 1))
        return tok

    def bcast(self, row_ap, n, dst_ap, dtype_bf16, waits, psl):
        one = self.ONERB if dtype_bf16 else self.ONERF
        mmtok = self.mmg(self.PSB[psl][:, 0:n], [(one[0:1, :], row_ap)],
                         waits + [self.ps_free[psl], self.one_r], inc=True)
        ev = self.P.v(lambda e, dst_ap=dst_ap, psl=psl, n=n: e.tensor_copy(
            dst_ap, self.PSB[psl][:, 0:n]), waits=[mmtok])
        self.ps_free[psl] = ev
        return ev

    def phase0(self):
        P, nc = self.P, self.nc
        xq_free = [None] * 4
        xtok = None
        for c in range(16):
            hq = c % 4
            q, qs = (("sync", "din"), ("scalar", "dis"))[c % 2]
            sl = self.STG[:, 1024 * hq:1024 * hq + 1024]
            ld = P.d(q, qs, lambda e, sl=sl, c=c: e.dma_start(
                sl, self.xT[128 * (c // 2):128 * (c // 2) + 128,
                            1024 * (c % 2):1024 * (c % 2) + 1024]),
                waits=[xq_free[hq]] if xq_free[hq] else [])
            xtok = P.v(lambda e, sl=sl, c=c: e.tensor_copy(
                self.XA[:, 2048 * (c // 2) + 1024 * (c % 2):
                        2048 * (c // 2) + 1024 * (c % 2) + 1024], sl),
                waits=[ld])
            xq_free[hq] = xtok
        self.stg_free = [xtok, xtok]
        self.x_ready = self.stg_free[(self.stage_i - 1) % 2]

        def load_w(dr, K, M, dst):
            tok = None
            for k in range(K // 128):
                tok = self.stage(dr[128 * k:128 * k + 128, :], M,
                                 lambda sl, ld, k=k: P.v(
                                     lambda e, sl=sl, k=k: e.tensor_copy(
                                         dst[:, M * k:M * k + M], sl), waits=[ld]))
            return tok
        self.wdq_r = load_w(self.wDQ, D, DL, self.WDQ)
        self.wdkv_r = load_w(self.wDKV, D, DL, self.WDKV)
        self.wkr_r = load_w(self.wKR, D, DHR, self.WKR)
        self.wqr_r = load_w(self.wQR, DL, H * DHR, self.WQR)
        self.wuq_r = load_w(self.wUQ, DL, D, self.WUQ)
        self.wuk_r = load_w(self.wUK, DL, D, self.WUK)
        self.wuv_r = load_w(self.wUV, DL, D, self.WUV)
        cp = lambda dst: (lambda sl, ld: P.v(
            lambda e, sl=sl, dst=dst: e.tensor_copy(dst, sl), waits=[ld]))
        self.rc_r = self.stage(self.ropeC[:, :], 2048, cp(self.RC[:, :]))
        self.rd_r = self.stage(self.ropeD[:, :], 2048, cp(self.RDm[:, :]))
        self.psw_r = self.stage(self.pswap[:, :], 128, cp(self.PSW[:, :]))
        self.tri_r = self.stage(self.trim[:, :], 1024, cp(self.TRI[:, :]))
        self.idt_r = self.stage(self.ident[:, :], 128, cp(self.IDT[:, :]))
        h_se = self.stage_i % 2
        self.stage_i += 1
        sl_se = self.STG[0:16, 2048 * h_se:2048 * h_se + 1024]
        ld_se = self.P.d("sync", "din",
                         lambda e, sl_se=sl_se: e.dma_start(sl_se, self.sel2[:, :]),
                         waits=[self.stg_free[h_se]] if self.stg_free[h_se] else [])
        self.sel_r = self.P.v(lambda e, sl_se=sl_se: e.tensor_copy(
            self.SELB[:, :], sl_se), waits=[ld_se])
        self.stg_free[h_se] = self.sel_r
        self.aft_r = self.stage3(self.aFCT.rearrange("(a p) r -> p a r", p=128),
                                 128, 8, cp(self.AFT[:, :]))
        self.apt_r = self.stage3(self.aPRT.rearrange("(a p) r -> p a r", p=128),
                                 512, 32, cp(self.APT[:, :]))
        self.one_r = P.v(lambda e: e.memset(self.ONEC[:, :], 1.0))
        self.one_r = P.v(lambda e: e.memset(self.ONERB[0:1, :], 1.0))
        self.one_r = P.v(lambda e: e.memset(self.ONERF[0:1, :], 1.0))

        def c_proj(dstT, wtile, wrd):
            ev = None
            for m in range(2):
                for nb in range(4):
                    sl = (m * 4 + nb) % 2
                    mmtok = self.mmg(
                        self.PSB[sl][:, 0:512],
                        [(wtile[:, DL * i + 128 * m: DL * i + 128 * m + 128],
                          self.XA[:, 2048 * i + 512 * nb: 2048 * i + 512 * nb + 512])
                         for i in range(8)],
                        [self.x_ready, wrd, self.ps_free[sl]])
                    ev = P.v(lambda e, m=m, nb=nb, sl=sl, dstT=dstT: e.tensor_copy(
                        dstT[:, 2048 * m + 512 * nb: 2048 * m + 512 * nb + 512],
                        self.PSB[sl][:, 0:512]), waits=[mmtok])
                    self.ps_free[sl] = ev
            return ev
        self.cq_r = c_proj(self.CQ, self.WDQ, self.wdq_r)
        self.ckv_r = c_proj(self.CKV, self.WDKV, self.wdkv_r)

        def rope(rows, dst_ap, raw_tok, extra=()):
            sw = None
            for nb in range(4):
                sl = 2 + nb % 2
                mmtok = self.mmg(
                    self.PSB[sl][0:rows, 0:512],
                    [(self.PSW[0:rows, 0:rows],
                      self.SC1[0:rows, 512 * nb:512 * nb + 512])],
                    [raw_tok, self.psw_r, self.ps_free[sl]] + list(extra))
                sw = P.v(lambda e, nb=nb, sl=sl, rows=rows: e.tensor_copy(
                    self.SC2[0:rows, 512 * nb:512 * nb + 512],
                    self.PSB[sl][0:rows, 0:512]), waits=[mmtok])
                self.ps_free[sl] = sw
            t1 = P.v(lambda e, rows=rows: e.tensor_mul(
                self.SC2[0:rows, 0:2048], self.SC2[0:rows, 0:2048],
                self.RDm[0:rows, :]), waits=[sw, self.rd_r])
            P.v(lambda e, rows=rows: e.tensor_mul(
                self.SC1[0:rows, 0:2048], self.SC1[0:rows, 0:2048],
                self.RC[0:rows, :]), waits=[self.rc_r])
            t3 = P.v(lambda e, rows=rows, dst_ap=dst_ap: e.tensor_add(
                dst_ap, self.SC1[0:rows, 0:2048], self.SC2[0:rows, 0:2048]),
                waits=[t1])
            return t3

        raw = None
        for nb in range(4):
            sl = nb % 2
            mmtok = self.mmg(
                self.PSB[sl][0:32, 0:512],
                [(self.WKR[:, DHR * i: DHR * i + DHR],
                  self.XA[:, 2048 * i + 512 * nb: 2048 * i + 512 * nb + 512])
                 for i in range(8)],
                [self.x_ready, self.wkr_r, self.ps_free[sl]])
            raw = P.v(lambda e, nb=nb, sl=sl: e.tensor_copy(
                self.SC1[0:32, 512 * nb:512 * nb + 512],
                self.PSB[sl][0:32, 0:512]), waits=[mmtok])
            self.ps_free[sl] = raw
        self.kr_rdy = rope(32, self.KRt[0:32, 0:2048], raw)

        self.qr_rdy = []
        for g in range(4):
            raw = None
            for nb in range(4):
                sl = nb % 2
                mmtok = self.mmg(
                    self.PSB[sl][:, 0:512],
                    [(self.WQR[:, (H * DHR) * l + 128 * g:
                               (H * DHR) * l + 128 * g + 128],
                      self.CQ[:, 2048 * l + 512 * nb: 2048 * l + 512 * nb + 512])
                     for l in range(2)],
                    [self.cq_r, self.wqr_r, self.ps_free[sl]])
                raw = P.v(lambda e, nb=nb, sl=sl: e.tensor_copy(
                    self.SC1[:, 512 * nb:512 * nb + 512], self.PSB[sl][:, 0:512]),
                    waits=[mmtok])
                self.ps_free[sl] = raw
            self.qr_rdy.append(
                rope(128, self.QRt[:, 2048 * g:2048 * g + 2048], raw))
        self.qr_all = self.qr_rdy[-1]

    def phase1(self):
        P = self.P
        self.vp_ms = P.v(lambda e: e.memset(self.VP[:, :], 1.0),
                         waits=[self.qr_rdy[-1]])
        for tc in range(16):
            for hf in range(2):
                sl = 2 + hf
                mmtok = self.mmg(
                    self.PSB[sl][:, 0:512],
                    [(self.CKV[:, 2048 * l + 128 * tc: 2048 * l + 128 * tc + 128],
                      self.WUV[:, D * l + 512 * hf: D * l + 512 * hf + 512])
                     for l in range(2)],
                    [self.ckv_r, self.wuv_r, self.ps_free[sl], self.vp_ms])
                base = 1040 * tc + 520 * hf
                dstv = self.VP[:, base:base + 520].rearrange(
                    "p (q f) -> p q f", q=8)
                srcv = self.PSB[sl][:, 0:512].rearrange("p (q f) -> p q f", q=8)
                ev1 = P.v(lambda e, dstv=dstv, srcv=srcv: e.tensor_copy(
                    dstv[:, :, 0:64], srcv[:, :, 0:64]), waits=[mmtok])
                self.ps_free[sl] = ev1
        self.x_done_t = (self.P.ch["tensor"], self.P.cnt["tensor"])

    def slot_wl(self, g):
        if g >= 3:
            return [self.slot_free.get(g - 3)]
        return [self.ps_free[2 * (g % 3)], self.ps_free[2 * (g % 3) + 1]]

    def emit_production(self, h, guard):
        for step in self.production_steps(h, guard):
            step()

    def production_steps(self, h, guard):
        P = self.P
        b = h % 2
        self.prod[h] = prod = {"kr": self.krk[b]}

        def q_step(nb):
            def go():
                g = self.rot_g
                self.rot_g += 1
                sl = 2 * (g % 3)
                mmtok = self.mmg(
                    self.PSB[sl][0:64, 0:512],
                    [(self.WUQ[:, D * l + DH * h: D * l + DH * h + DH],
                      self.CQ[:, 2048 * l + 512 * nb: 2048 * l + 512 * nb + 512])
                     for l in range(2)],
                    [self.cq_r, self.wuq_r] + self.slot_wl(g) + guard)
                qev = P.s(lambda e, b=b, nb=nb, sl=sl: e.copy(
                    self.QHB[b][0:64, 512 * nb:512 * nb + 512],
                    self.PSB[sl][0:64, 0:512]), waits=[mmtok] + guard)
                self.slot_free[g] = qev
                prod["q"] = qev
            return go

        def k_step(nb):
            def go():
                g = self.rot_g
                self.rot_g += 1
                sl = 2 * (g % 3)
                mmtok = self.mmg(
                    self.PSB[sl][0:64, 0:512],
                    [(self.WUK[:, D * l + DH * h: D * l + DH * h + DH],
                      self.CKV[:, 2048 * l + 512 * nb: 2048 * l + 512 * nb + 512])
                     for l in range(2)],
                    [self.ckv_r, self.wuk_r] + self.slot_wl(g) + guard)
                kev = P.s(lambda e, b=b, nb=nb, sl=sl: e.copy(
                    self.KHB[b][0:64, 512 * nb:512 * nb + 512],
                    self.PSB[sl][0:64, 0:512]), waits=[mmtok] + guard)
                self.slot_free[g] = kev
                prod["k"] = kev
            return go

        def qr_step():
            def go():
                prod["qr"] = P.d("scalar", "dqr", lambda e, h=h, b=b: e.dma_start(
                    self.QHB[b][64:96, 0:2048],
                    self.QRt[32 * (h % 4):32 * (h % 4) + 32,
                             2048 * (h // 4):2048 * (h // 4) + 2048]),
                    waits=[self.qr_all, prod["q"], self.x_done_t] + guard)
            return go

        return ([q_step(nb) for nb in range(4)] + [qr_step()]
                + [k_step(nb) for nb in range(4)])

    def emit_absmax_dma(self, ci):
        P = self.P
        fc = ci < 32
        mat = self.wFCr if fc else self.wPRr
        j = ci % 32
        h4 = ci % 4
        sl = self.STG[:, 1024 * h4:1024 * h4 + 1024]
        sl3 = sl.rearrange("p (b o) -> p b o", b=2)
        ld = P.d("gpsimd", "dax", lambda e, sl3=sl3, mat=mat, j=j:
                 e.dma_start(sl3, mat[256 * j:256 * j + 256, :].rearrange(
                     "(b p) o -> p b o", p=128)),
            waits=[self.stg_free[h4]] if self.stg_free[h4] else [])
        self.ax_ld[ci] = ld

    def emit_absmax_reduce(self, ci):
        P = self.P
        h4 = ci % 4
        sl = self.STG[:, 1024 * h4:1024 * h4 + 1024]
        rd = P.v(lambda e, ci=ci, sl=sl: e.tensor_reduce(
            self.MXW[:, ci:ci + 1], sl, AXX, ALU.max,
            apply_absolute_value=True), waits=[self.ax_ld[ci]])
        self.stg_free[h4] = rd
        self.absmax_last = rd

    def phase2(self):
        P = self.P
        self.prod = {}
        self.last_prod_toks = []
        self.krk = [P.d("scalar", "dqr", lambda e, b=b: e.dma_start(
            self.KHB[b][64:96, 0:2048], self.KRt[0:32, 0:2048]),
            waits=[self.kr_rdy, self.qr_all, self.x_done_t]) for b in range(2)]
        self.rot_g = 0
        self.slot_free = {}
        self.ax_ld = {}
        self.stg_free = [self.stg_free[0], self.stg_free[0],
                         self.stg_free[1], self.stg_free[1]]
        self.emit_production(0, [])
        prod_pending = self.production_steps(1, [])
        for ci in range(4):
            self.emit_absmax_dma(ci)

        clip_pair = {}
        exp_pair = {}
        self.pv_tok = {}
        bigfree = []
        dn_dma = {}
        last_sc = {}
        odsc_free = []
        ti = 0

        def emit_pv(item):
            (i, h, qb, kt, qlo, qext, cs, off, ready, first_of_bank) = item
            pcol = 1040 * kt + 65 * h
            w = [ready]
            if first_of_bank:
                if qb >= 2:
                    w += self.half_free.get((h, qb - 2), [])
                elif h >= 1:
                    w += self.half_free.get((h - 1, qb + 2), [])
            lq = qlo - 512 * qb + 512 * (qb % 2)
            tok = P.t(lambda e, pcol=pcol, lq=lq, qext=qext, cs=cs, off=off,
                      qb=qb, kt=kt: e.matmul(
                          self.BIG[0:65, lq:lq + qext],
                          self.VP[:, pcol:pcol + 65],
                          self.PTB[cs][:, off:off + qext],
                          start=(kt == 0), stop=(kt == 4 * qb + 3)),
                      waits=w, inc=True)
            self.pv_tok[i] = tok
            if kt == 4 * qb + 3:
                emit_bank_evac(item, tok)

        def emit_bank_evac(item, bpv):
            (i, h, qb, kt, qlo, qext, cs, off, ready, fob) = item
            even = (h % 2 == 0)
            hq = 512 * (qb % 2)
            dtok = P.s(lambda e, qb=qb, hq=hq: e.copy(
                self.DRW[64:65, 512 * qb:512 * qb + 512],
                self.BIG[64:65, hq:hq + 512]),
                waits=[bpv, dn_dma.get(h - 1)])
            if even:
                aev = P.v(lambda e, h=h, qb=qb, hq=hq: e.tensor_copy(
                    self.XA[0:64, 2048 * (h // 2) + 512 * qb:
                            2048 * (h // 2) + 512 * qb + 512],
                    self.BIG[0:64, hq:hq + 512]),
                    waits=[bpv, self.x_done_t])
            else:
                aev = P.v(lambda e, qb=qb, hq=hq: e.tensor_copy(
                    self.ODSC[0:64, 512 * qb:512 * qb + 512],
                    self.BIG[0:64, hq:hq + 512]),
                    waits=[bpv, odsc_free[0] if odsc_free else None])
            self.half_free[(h, qb)] = [dtok, aev]
            head_evacs.setdefault(h, []).append((dtok, aev))

        self.half_free = {}
        self.xa_rdy = {}
        head_evacs = {}
        for h in range(16):
            b = h % 2
            even = (h % 2 == 0)
            pw = self.prod[h]
            tiles = [(qb, kt) for qb in range(4) for kt in range(4 * qb + 4)]
            pend = []
            for idx in range(0, len(tiles), 2):
                pj = (ti + idx) // 2
                g = self.rot_g
                self.rot_g += 1
                pslot = g % 3
                cs = pj % 4
                if idx == len(tiles) // 2:
                    self.emit_absmax_reduce(4 * h)
                    self.emit_absmax_reduce(4 * h + 1)
                    if 4 * h + 4 < 64:
                        self.emit_absmax_dma(4 * h + 4)
                    if 4 * h + 5 < 64:
                        self.emit_absmax_dma(4 * h + 5)
                exts = []
                stok = None
                for half in range(2):
                    qb, kt = tiles[idx + half]
                    qlo = max(512 * qb, 128 * kt)
                    qext = 512 * qb + 512 - qlo
                    exts.append((qb, kt, qlo, qext))
                    w = []
                    if idx == 0:
                        w += [pw["q"], pw["k"], pw["qr"], pw["kr"]]
                    if half == 0:
                        w += self.slot_wl(g)
                    stok = P.t(lambda e, pslot=pslot, kt=kt, qlo=qlo,
                               qext=qext, b=b, half=half:
                               e.matmul(
                                   self.PSD[pslot][:, 512 * half:
                                                   512 * half + qext],
                                   self.KHB[b][0:96, 128 * kt:128 * kt + 128],
                                   self.QHB[b][0:96, qlo:qlo + qext],
                                   start=True, stop=True), waits=w, inc=True)
                qw = 512 + exts[1][3]
                if exts[1][1] >= 4 * exts[1][0]:
                    ctok = P.v(lambda e, pslot=pslot, cs=cs, qw=qw:
                               e.scalar_tensor_tensor(
                                   self.CLB[cs][:, 0:qw],
                                   self.PSD[pslot][:, 0:qw], -80.0,
                                   self.TRI[:, 0:qw], ALU.max, ALU.min),
                               waits=[stok, exp_pair.get(pj - 4), self.tri_r])
                else:
                    ctok = P.v(lambda e, pslot=pslot, cs=cs, qw=qw:
                               e.tensor_scalar(
                                   self.CLB[cs][:, 0:qw],
                                   self.PSD[pslot][:, 0:qw],
                                   80.0, -80.0, ALU.min, ALU.max),
                               waits=[stok, exp_pair.get(pj - 4)])
                clip_pair[pj] = ctok
                self.slot_free[g] = ctok
                etok = P.s(lambda e, cs=cs, qw=qw: e.activation(
                    self.PTB[cs][:, 0:qw], self.CLB[cs][:, 0:qw],
                    AF.Exp, scale=ISQ),
                    waits=[ctok, self.pv_tok.get(2 * pj - 7)])
                exp_pair[pj] = etok
                for half in range(2):
                    qb, kt, qlo, qext = exts[half]
                    pend.append((ti + idx + half, h, qb, kt, qlo, qext,
                                 cs, 512 * half, etok, kt == 0))
                while len(pend) > 6:
                    emit_pv(pend.pop(0))
                if prod_pending and idx >= 2:
                    prod_pending.pop(0)()
            while pend:
                emit_pv(pend.pop(0))
            while prod_pending:
                prod_pending.pop(0)()
            last_sc[h] = stok
            ti += len(tiles)

            aev_l = [a for (d, a) in head_evacs.get(h, [])]
            if not even:
                od = P.d("sync", "dod", lambda e, h=h: e.dma_start(
                    self.XA[64:128, 2048 * (h // 2):2048 * (h // 2) + 2048],
                    self.ODSC[0:64, 0:2048]), waits=aev_l + [self.x_done_t])
                odsc_free[:] = [od]
                self.xa_rdy.setdefault(h // 2, []).append(od)
            else:
                self.xa_rdy.setdefault(h // 2, []).append(aev_l[-1])
            dn_dma[h] = P.d("sync", "ddn", lambda e, h=h: e.dma_start(
                self.DNT[h:h + 1, 0:2048], self.DRW[64:65, 0:2048]),
                waits=[d for (d, a) in head_evacs.get(h, [])])
            bigfree = [t for pair in head_evacs.get(h, []) for t in pair]
            if h + 2 <= 15:
                prod_pending = self.production_steps(h + 2, [last_sc[h]])
            if h == 14:
                early = [self.prod[15]["q"], self.prod[15]["k"]] \
                    + P.stream_now("dqr")
                self.x2_tok = []
                for d in range(self.x2_early_n):
                    q = "scalar" if d % 2 == 0 else "gpsimd"
                    self.x2_tok.append(P.d(q, "dx2", lambda e, d=d: e.dma_start(
                        self.XFM[:, 2048 * d:2048 * d + 2048],
                        self.xT[128 * d:128 * d + 128, :]), waits=early))
            self.emit_absmax_reduce(4 * h + 2)
            self.emit_absmax_reduce(4 * h + 3)
            if 4 * h + 6 < 64:
                self.emit_absmax_dma(4 * h + 6)
            if 4 * h + 7 < 64:
                self.emit_absmax_dma(4 * h + 7)
        self.dn_last = dn_dma[15]
        self.attn_evac_last = list(bigfree)
        self.last_sc = last_sc
        for j in range(3):
            g = self.rot_g - 1 - j
            self.ps_free[2 * (g % 3)] = self.slot_free[g]
            self.ps_free[2 * (g % 3) + 1] = self.slot_free[g]

    def quant_setup(self):
        self.rtf_v = self.RTF[:, :].rearrange("p (i o) -> p i o", i=8)
        self.rtp_v = self.RTP[:, :].rearrange("p (f o) -> p f o", f=32)
        self.quant_sched = []
        self.qtok_fc = [None] * 32
        self.qtok_pr = {}
        for oc in range(32):
            for half in range(2):
                col = 128 * oc + 64 * half
                cfc = 2 * oc + half
                self.quant_sched.append(
                    ("fc", oc, self.wFCr[128 * cfc:128 * cfc + 128, :],
                     self.rtf_v[:, :, col:col + 64], 0))
        for dc in range(8):
            for r in range(4):
                for half in range(2):
                    col = 128 * dc + 64 * half
                    cpr = 16 * r + 2 * dc + half
                    self.quant_sched.append(
                        ("pr", dc, self.wPRr[128 * cpr:128 * cpr + 128, :],
                         self.rtp_v[:, 8 * r:8 * r + 8, col:col + 64], 3))
        self.qn = 0
        self.q_prev = {}
        self.q_inflight = []
        self.half_r = self.P.v(lambda e: e.memset(self.SML[:, 6:7], 0.5))

    def _quant_front(self):
        P = self.P
        kind, idx, src_ap, dst_view, smoff = self.quant_sched.pop(0)
        assert kind == "fc" or self.q_rtp_free, "proj quant before RTP region free"
        c = self.qn
        self.qn += 1
        qsg = self.QSG[c % 3]
        qti = self.QTI[c % 2]
        qsg3 = qsg[:, 0:512].rearrange("p (a o) -> p a o", a=8)
        w_dma = [self.q_prev.get(("p4", c - 3))]
        if c < 4:
            w_dma += self.q_rt_free + self.attn_evac_last
        qq, qs = ((("gpsimd", "dqw"), ("scalar", "dqs"))[c % 2] if c < 32
                  else (("gpsimd", "dqw"), ("scalar", "dqs"),
                        ("sync", "dqy"))[c % 3])
        ld = P.d(qq, qs,
                 lambda e, qsg=qsg, src_ap=src_ap: e.dma_start(
                     qsg[:, 0:512], src_ap),
                 waits=w_dma)
        p1 = P.s(lambda e, qsg=qsg, smoff=smoff: e.activation(
            qsg[:, 0:512], qsg[:, 0:512], AF.Identity,
            bias=self.SML[:, 6:7], scale=self.SML[:, smoff:smoff + 1]),
            waits=[ld, self.sml_tok, self.half_r])
        p2 = P.s(lambda e, qsg=qsg, qti=qti: e.copy(
            qti[:, 0:512], qsg[:, 0:512]),
            waits=[p1, self.q_prev.get(("p4", c - 2))])
        self.q_prev[("p2", c)] = p2
        self.q_inflight.append((c, kind, idx, qsg, qti, qsg3, dst_view, smoff))

    def _quant_back(self):
        P = self.P
        (c, kind, idx, qsg, qti, qsg3, dst_view, smoff) = self.q_inflight.pop(0)
        p3 = P.v(lambda e, qsg=qsg, qti=qti: e.tensor_tensor(
            qsg[:, 0:512], qsg[:, 0:512], qti[:, 0:512], ALU.is_lt),
            waits=[self.q_prev[("p2", c)]])
        w4 = [p3] + self.q_rt_free
        if kind == "pr":
            w4 += self.q_rtp_free
        qti3 = qti[:, 0:512].rearrange("p (a o) -> p a o", a=8)
        p4 = P.v(lambda e, qsg3=qsg3, qti3=qti3, dst_view=dst_view:
                 e.tensor_tensor(dst_view, qti3, qsg3, ALU.subtract),
                 waits=w4)
        self.q_prev[("p4", c)] = p4
        if kind == "fc":
            self.qtok_fc[idx] = p4
        else:
            self.qtok_pr[idx] = p4

    def emit_quant_chunk(self, allow_pr=True):
        if len(self.q_inflight) >= 2 or (not self.quant_sched and self.q_inflight):
            self._quant_back()
        if self.quant_sched and (allow_pr or self.quant_sched[0][0] == "fc"):
            self._quant_front()
    

    def phase3(self):
        P = self.P
        mr0 = P.v(lambda e: e.tensor_reduce(
            self.MXR[:, 0:1], self.MXW[:, 0:32], AXX, ALU.max),
            waits=[self.absmax_last])
        mr1 = P.v(lambda e: e.tensor_reduce(
            self.MXR[:, 1:2], self.MXW[:, 32:64], AXX, ALU.max), waits=[])
        mrd = P.d("sync", "dsm", lambda e: e.dma_start(
            self.DRW[0:1, 1792:2048], self.MXR[:, 0:2]), waits=[mr0, mr1])
        gtok = None
        for m in range(2):
            base = 8 * m
            P.v(lambda e, m=m, base=base: e.tensor_reduce(
                self.MX1[0:1, base:base + 1],
                self.DRW[0:1, 1792:2048].rearrange("a (p b) -> a p b", b=2)
                [:, :, m].rearrange("a p -> a p"),
                AXX, ALU.max), waits=[mrd])
            P.v(lambda e, base=base: e.reciprocal(
                self.MX1[0:1, base + 1:base + 2], self.MX1[0:1, base:base + 1]))
            P.v(lambda e, base=base: e.tensor_mul(
                self.MX1[0:1, base + 2:base + 3], self.MX1[0:1, base:base + 1],
                self.MX1[0:1, base + 1:base + 2]))
            P.v(lambda e, base=base: e.tensor_scalar(
                self.MX1[0:1, base + 2:base + 3], self.MX1[0:1, base + 2:base + 3],
                -1.0, 2.0, ALU.mult, ALU.add))
            P.v(lambda e, base=base: e.tensor_mul(
                self.MX1[0:1, base + 1:base + 2], self.MX1[0:1, base + 1:base + 2],
                self.MX1[0:1, base + 2:base + 3]))
            P.v(lambda e, base=base, m=m: e.tensor_scalar(
                self.MX1[0:1, 32 + 3 * m:33 + 3 * m],
                self.MX1[0:1, base + 1:base + 2], 127.0, None, ALU.mult))
            P.v(lambda e, base=base, m=m: e.tensor_scalar(
                self.MX1[0:1, 33 + 3 * m:34 + 3 * m],
                self.MX1[0:1, base + 1:base + 2], 254.0, None, ALU.mult))
            gtok = P.v(lambda e, base=base, m=m: e.tensor_scalar(
                self.MX1[0:1, 34 + 3 * m:35 + 3 * m],
                self.MX1[0:1, base:base + 1], 1.0 / 127.0, None, ALU.mult))
        self.sml_tok = self.bcast(self.MX1[0:1, 32:38], 6, self.SML[:, 0:6],
                                  False, [gtok], 0)

        ln1 = P.s(lambda e: e.activation(self.DNT[:, :], self.DNT[:, :], AF.Ln),
                  waits=[self.dn_last])
        ex1 = P.s(lambda e: e.activation(self.DNT[:, :], self.DNT[:, :], AF.Exp,
                                         scale=-1.0), waits=[ln1])
        dcb = P.v(lambda e: e.tensor_copy(self.DNTB[:, :], self.DNT[:, :]),
                  waits=[ex1])
        self.norm_tok = {}
        for c in range(8):
            rb = self.RB2[:, 0:2048]
            evs = []
            for nb in range(4):
                sl = 2 + nb % 2
                mmtok = self.mmg(
                    self.PSB[sl][:, 0:512],
                    [(self.SELB[0:16, 128 * c:128 * c + 128],
                      self.DNTB[0:16, 512 * nb:512 * nb + 512])],
                    [dcb, self.sel_r, self.ps_free[sl],
                     self.norm_tok.get(c - 1)])
                ev = P.v(lambda e, rb=rb, nb=nb, sl=sl: e.tensor_copy(
                    rb[:, 512 * nb:512 * nb + 512], self.PSB[sl][:, 0:512]),
                    waits=[mmtok])
                self.ps_free[sl] = ev
                evs.append(ev)
            self.norm_tok[c] = P.v(lambda e, c=c, rb=rb: e.tensor_mul(
                self.XA[:, 2048 * c:2048 * c + 2048],
                self.XA[:, 2048 * c:2048 * c + 2048], rb),
                waits=evs + self.xa_rdy[c] + self.attn_evac_last)

        attn_done = (self.P.stream_now("dqr") + self.P.stream_now("dod")
                     + self.P.stream_now("ddn")
                     + [self.last_sc[15], self.absmax_last])
        x2_tok = self.x2_tok
        for d in range(self.x2_early_n, 8):
            q = "scalar" if d % 2 == 0 else "gpsimd"
            x2_tok.append(P.d(q, "dx2", lambda e, d=d: e.dma_start(
                self.XFM[:, 2048 * d:2048 * d + 2048],
                self.xT[128 * d:128 * d + 128, :]), waits=attn_done))
        self.quant_setup()
        self.q_rt_free = attn_done
        self.q_rtp_free = []
        self.y_tok = {}
        wo_cast_prev = {}
        wo_mm_last = {}
        for dslice in range(8):
            wh = dslice % 2
            ldw = P.d("sync", "dws", lambda e, dslice=dslice, wh=wh: e.dma_start(
                self.WST[:, 1024 * wh:1024 * wh + 1024],
                self.wOr[128 * dslice:128 * dslice + 128, :]),
                waits=[wo_cast_prev.get(dslice - 2)]
                      + (attn_done if dslice < 2 else []))
            wo_cast = P.s(lambda e, wh=wh: e.copy(
                self.WOBs[:, :],
                self.WST[:, 1024 * wh:1024 * wh + 1024]),
                waits=[ldw, wo_mm_last.get(dslice - 1),
                       self.absmax_last if dslice == 0 else None])
            wo_cast_prev[dslice] = wo_cast
            for nb in range(4):
                sl = (dslice * 4 + nb) % 4
                wmm = [wo_cast, self.ps_free[sl]]
                if dslice == 0 and nb == 0:
                    wmm += list(self.norm_tok.values())
                mmtok = self.mmg(
                    self.PSB[sl][:, 0:512],
                    [(self.WOBs[:, 128 * hc:128 * hc + 128],
                      self.XA[:, 2048 * hc + 512 * nb: 2048 * hc + 512 * nb + 512])
                     for hc in range(8)],
                    wmm)
                wo_mm_last[dslice] = mmtok
                ytok = P.v(lambda e, dslice=dslice, nb=nb, sl=sl: e.tensor_add(
                    self.XFM[:, 2048 * dslice + 512 * nb:
                             2048 * dslice + 512 * nb + 512],
                    self.PSB[sl][:, 0:512],
                    self.XFM[:, 2048 * dslice + 512 * nb:
                             2048 * dslice + 512 * nb + 512]),
                    waits=[mmtok, x2_tok[dslice]])
                self.ps_free[sl] = ytok
                self.y_tok[(dslice, nb)] = ytok
                self.emit_quant_chunk()
        self.wo_done = (self.P.ch["tensor"], self.P.cnt["tensor"])

    def phase4(self):
        P = self.P
        sy_last = None
        sq_last = None
        for dslice in range(8):
            w = [self.y_tok[(dslice, nb)] for nb in range(4)]
            if sq_last:
                w.append(sq_last)
            cast_tok = P.v(lambda e, dslice=dslice: e.tensor_copy(
                self.YC1[:, :], self.XFM[:, 2048 * dslice:2048 * dslice + 2048]),
                waits=w)
            sq_t = [self.PSB[4][0:1, 0:512], self.PSB[5][0:1, 0:512],
                    self.BIG[0:1, 0:512], self.BIG[0:1, 512:1024]]
            for nb in range(4):
                wmm = [cast_tok, self.one_r]
                if dslice == 0:
                    wmm.append(self.ps_free[nb])
                sy_last = P.t(lambda e, nb=nb, st=(dslice == 0), sp=(dslice == 7):
                              e.matmul(self.PSB[nb][0:1, 0:512],
                                       self.ONEC[:, 0:1],
                                       self.YC1[:, 512 * nb:512 * nb + 512],
                                       start=st, stop=sp),
                              waits=wmm, inc=True)
            sqc = P.s(lambda e: e.activation(self.YC1[:, :], self.YC1[:, :],
                                             AF.Square), waits=[cast_tok, sy_last])
            self.emit_quant_chunk(allow_pr=False)
            self.emit_quant_chunk(allow_pr=False)
            for nb in range(4):
                wmm = [sqc]
                if dslice == 0:
                    wmm += self.attn_evac_last
                    if nb < 2:
                        wmm.append(self.ps_free[4 + nb])
                sq_last = P.t(lambda e, nb=nb, st=(dslice == 0), sp=(dslice == 7),
                              tgt=sq_t[nb]:
                              e.matmul(tgt, self.ONEC[:, 0:1],
                                       self.YC1[:, 512 * nb:512 * nb + 512],
                                       start=st, stop=sp),
                              waits=wmm, inc=True)
        ev1 = None
        for nb in range(4):
            ev1 = P.s(lambda e, nb=nb: e.copy(
                self.STY[0:1, 512 * nb:512 * nb + 512], self.PSB[nb][0:1, 0:512]),
                waits=[sy_last])
            self.ps_free[nb] = ev1
        ev2 = None
        for nb in range(4):
            ev2 = P.s(lambda e, nb=nb, tgt=sq_t[nb]: e.copy(
                self.STY2[0:1, 512 * nb:512 * nb + 512], tgt),
                waits=[sq_last])
        self.big_free_ln1 = ev2
        mtok = P.v(lambda e: e.tensor_scalar(
            self.STY[0:1, :], self.STY[0:1, :], 1.0 / 1024.0, None, ALU.mult),
            waits=[ev1])
        P.v(lambda e: e.tensor_scalar(
            self.STY2[0:1, :], self.STY2[0:1, :], 1.0 / 1024.0, None, ALU.mult),
            waits=[ev2])
        P.v(lambda e: e.tensor_mul(self.STY[0:1, 0:S].broadcast_to((1, S)) if False
                                   else self.YC1[0:1, 0:S].bitcast(F32) if False
                                   else self.MBC[0:1, 0:1024], self.STY[0:1, 0:1024],
                                   self.STY[0:1, 0:1024]))
        P.v(lambda e: e.tensor_sub(self.STY2[0:1, 0:1024], self.STY2[0:1, 0:1024],
                                   self.MBC[0:1, 0:1024]))
        P.v(lambda e: e.tensor_mul(self.MBC[0:1, 0:1024], self.STY[0:1, 1024:2048],
                                   self.STY[0:1, 1024:2048]))
        P.v(lambda e: e.tensor_sub(self.STY2[0:1, 1024:2048],
                                   self.STY2[0:1, 1024:2048],
                                   self.MBC[0:1, 0:1024]))
        vtok = P.v(lambda e: e.tensor_scalar(
            self.STY2[0:1, :], self.STY2[0:1, :], 1e-5, None, ALU.add))
        lt = P.s(lambda e: e.activation(self.STY2[0:1, :], self.STY2[0:1, :],
                                        AF.Ln), waits=[vtok])
        rt = P.s(lambda e: e.activation(self.STY2[0:1, :], self.STY2[0:1, :],
                                        AF.Exp, scale=-0.5), waits=[lt])
        self.xt_rdy = {}
        apl = {}
        for nb in range(4):
            hb = nb % 2
            mb = self.bcast(self.STY[0:1, 512 * nb:512 * nb + 512], 512,
                            self.MBC[:, 512 * hb:512 * hb + 512], False,
                            [mtok, apl.get((nb - 2, 7))], 2 + nb % 2)
            rb = self.bcast(self.STY2[0:1, 512 * nb:512 * nb + 512], 512,
                            self.RBC[:, 512 * hb:512 * hb + 512], False,
                            [rt, apl.get((nb - 2, 7))], 2 + nb % 2)
            for dslice in range(8):
                P.v(lambda e, dslice=dslice, nb=nb, hb=hb: e.tensor_sub(
                    self.XFM[:, 2048 * dslice + 512 * nb:
                             2048 * dslice + 512 * nb + 512],
                    self.XFM[:, 2048 * dslice + 512 * nb:
                             2048 * dslice + 512 * nb + 512],
                    self.MBC[:, 512 * hb:512 * hb + 512]), waits=[mb, rb])
                apl[(nb, dslice)] = P.v(
                    lambda e, dslice=dslice, nb=nb, hb=hb: e.tensor_mul(
                        self.XT16[:, 2048 * dslice + 512 * nb:
                                 2048 * dslice + 512 * nb + 512],
                        self.XFM[:, 2048 * dslice + 512 * nb:
                                 2048 * dslice + 512 * nb + 512],
                        self.RBC[:, 512 * hb:512 * hb + 512]), waits=[])
        for dslice in range(8):
            self.xt_rdy[dslice] = apl[(3, dslice)]
        self.ph4_bar = [self.xt_rdy[7], rt, sq_last, self.absmax_last]
        self.q_rtp_free = list(self.xt_rdy.values()) + [self.wo_done]

    def phase5(self):
        P = self.P
        xt_all = list(self.xt_rdy.values())
        bar = list(self.ph4_bar)
        self.q_rtp_free = xt_all + [self.wo_done]
        while self.q_inflight:
            self._quant_back()
        bfcs_ld = P.d("sync", "dfb", lambda e: e.dma_start(
            self.BFCS[:, :], self.bFC[:, :]), waits=xt_all + bar)
        bprs_ld = P.d("sync", "dfb", lambda e: e.dma_start(
            self.BPRS[:, :], self.bPR[:, :]), waits=xt_all + bar)
        bsf_tok = bsp_tok = None
        for c in range(10):
            pb = self.qn % 3
            self.qn += 1
            qsg = self.QSG[pb]
            src_ap = (self.bFCbT[:, 512 * c:512 * c + 512] if c < 8
                      else self.bPRbT[:, 512 * (c - 8):512 * (c - 8) + 512])
            dst_ap = (self.BSF[:, 512 * c:512 * c + 512] if c < 8
                      else self.BSP[:, 512 * (c - 8):512 * (c - 8) + 512])
            ld = P.d("sync", "dfb", lambda e, qsg=qsg, src_ap=src_ap: e.dma_start(
                qsg[0:16, 0:512], src_ap),
                waits=[self.q_prev.get(("p4", self.qn - 3))] + xt_all + bar)
            tok = P.v(lambda e, qsg=qsg, dst_ap=dst_ap: e.tensor_copy(
                dst_ap, qsg[0:16, 0:512]), waits=[ld] + bar)
            self.q_prev[("p4", self.qn - 1)] = tok
            if c < 8:
                bsf_tok = tok
            else:
                bsp_tok = tok

        ufc_tok = None
        for nb in range(4):
            sl = nb % 2
            mmtok = self.mmg(
                self.PSB[sl][0:16, 0:512],
                [(self.AFT[:, 16 * i:16 * i + 16],
                  self.XT16[:, 2048 * i + 512 * nb: 2048 * i + 512 * nb + 512])
                 for i in range(8)],
                [self.aft_r, self.ps_free[sl]] + xt_all + bar)
            ufc_tok = P.v(lambda e, nb=nb, sl=sl: e.tensor_scalar(
                self.UO[0:16, 512 * nb:512 * nb + 512], self.PSB[sl][0:16, 0:512],
                self.SML[0:16, 1:2], None, ALU.mult), waits=[mmtok, self.sml_tok])
            self.ps_free[sl] = ufc_tok

        for _ in range(24):
            if self.quant_sched or self.q_inflight:
                self.emit_quant_chunk()

        rtf_v = self.rtf_v
        rtp_v = self.rtp_v
        qtok_fc = self.qtok_fc
        qtok_pr = self.qtok_pr
        bar = list(self.ph4_bar)
        ost_free = [None, None]
        hh_tok = {}
        y2_tok = {}
        out_dma = []
        sqmm = None
        e1_prev = e2_prev = None
        for t in range(NTC):
            tbar = bar if t == 0 else []
            for oc in range(32):
                sl = oc % 2
                parts = [(rtf_v[:, i, 128 * oc:128 * oc + 128],
                          self.XT16[:, 2048 * i + TC * t: 2048 * i + TC * t + TC])
                         for i in range(8)]
                parts.append((self.BSF[:, 128 * oc:128 * oc + 128],
                              self.UO[0:16, TC * t:TC * t + TC]))
                mmtok = self.mmg(self.PSB[sl][:, 0:TC], parts,
                                 [qtok_fc[oc], bsf_tok, ufc_tok,
                                  self.ps_free[sl]] + xt_all + tbar)
                hh_tok[oc] = P.s(lambda e, oc=oc, sl=sl: e.activation(
                    self.HH[:, TC * oc:TC * oc + TC], self.PSB[sl][:, 0:TC],
                    AF.Gelu, bias=self.BFCS[:, oc:oc + 1],
                    scale=self.SML[:, 2:3]),
                    waits=[mmtok, bfcs_ld, self.sml_tok] + tbar)
                self.ps_free[sl] = hh_tok[oc]
                upw = [hh_tok[oc], self.apt_r]
                if oc == 0:
                    upw.append(self.ps_free[2])
                upmm = P.t(lambda e, oc=oc, t=t: e.matmul(
                    self.PSB[2][0:16, 0:TC],
                    self.APT[:, 16 * oc:16 * oc + 16],
                    self.HH[:, TC * oc:TC * oc + TC],
                    start=(oc == 0), stop=(oc == 31)),
                    waits=upw, inc=(oc == 31))
                if self.quant_sched or self.q_inflight:
                    self.emit_quant_chunk()
                    self.emit_quant_chunk()
            while self.quant_sched or self.q_inflight:
                self.emit_quant_chunk()
            up_tok = P.v(lambda e: e.tensor_scalar(
                self.UPt[0:16, 0:TC], self.PSB[2][0:16, 0:TC],
                self.SML[0:16, 4:5], None, ALU.mult), waits=[upmm])
            self.ps_free[2] = up_tok
            for dc in range(8):
                sl = 2 + dc % 2
                parts = [(rtp_v[:, f, 128 * dc:128 * dc + 128],
                          self.HH[:, TC * f:TC * f + TC]) for f in range(32)]
                parts.append((self.BSP[:, 128 * dc:128 * dc + 128],
                              self.UPt[0:16, 0:TC]))
                mmtok = self.mmg(self.PSB[sl][:, 0:TC], parts,
                                 [qtok_pr[dc], bsp_tok, up_tok,
                                  self.ps_free[sl]])
                ph = dc % 2
                pa = P.s(lambda e, dc=dc, sl=sl, ph=ph: e.activation(
                    self.PTMP[:, TC * ph:TC * ph + TC], self.PSB[sl][:, 0:TC],
                    AF.Identity,
                    bias=self.BPRS[:, dc:dc + 1], scale=self.SML[:, 5:6]),
                    waits=[mmtok, bprs_ld, self.sml_tok,
                           y2_tok.get((dc - 2) % 8)] + tbar)
                y2_tok[dc] = P.v(lambda e, dc=dc, t=t, ph=ph: e.tensor_add(
                    self.Y2C[:, TC * dc:TC * dc + TC],
                    self.PTMP[:, TC * ph:TC * ph + TC],
                    self.XT16[:, 2048 * dc + TC * t: 2048 * dc + TC * t + TC]),
                    waits=[pa, e2_prev] + tbar)
                self.ps_free[sl] = y2_tok[dc]
            symm = None
            for dc in range(8):
                wsy = [y2_tok[dc], self.one_r]
                if dc == 0:
                    wsy.append(e1_prev if t else self.big_free_ln1)
                symm = P.t(lambda e, dc=dc, st=(dc == 0), sp=(dc == 7): e.matmul(
                    self.BIG[0:1, 0:TC], self.ONEC[:, 0:1],
                    self.Y2C[:, TC * dc:TC * dc + TC], start=st, stop=sp),
                    waits=wsy, inc=(dc == 7))
                sq = P.s(lambda e, dc=dc: e.activation(
                    self.TSQ[:, 0:TC], self.Y2C[:, TC * dc:TC * dc + TC],
                    AF.Square), waits=[y2_tok[dc], sqmm])
                wsq = [sq]
                if dc == 0:
                    wsq.append(e2_prev if t else self.big_free_ln1)
                sqmm = P.t(lambda e, dc=dc, st=(dc == 0), sp=(dc == 7): e.matmul(
                    self.BIG[0:1, 512:512 + TC], self.ONEC[:, 0:1],
                    self.TSQ[:, 0:TC], start=st, stop=sp),
                    waits=wsq, inc=True)
            e1 = P.s(lambda e: e.copy(self.L2R[0][0:1, :], self.BIG[0:1, 0:TC]),
                     waits=[symm])
            e2 = P.s(lambda e: e.copy(self.L2R[1][0:1, :], self.BIG[0:1, 512:512 + TC]),
                     waits=[sqmm])
            e1_prev, e2_prev = e1, e2
            mt = P.v(lambda e: e.tensor_scalar(
                self.L2R[0][0:1, :], self.L2R[0][0:1, :], 1.0 / 1024.0, None,
                ALU.mult), waits=[e1])
            P.v(lambda e: e.tensor_scalar(
                self.L2R[1][0:1, :], self.L2R[1][0:1, :], 1.0 / 1024.0, None,
                ALU.mult), waits=[e2])
            P.v(lambda e: e.tensor_mul(self.L2R[2][0:1, :], self.L2R[0][0:1, :],
                                       self.L2R[0][0:1, :]))
            P.v(lambda e: e.tensor_sub(self.L2R[1][0:1, :], self.L2R[1][0:1, :],
                                       self.L2R[2][0:1, :]))
            vt = P.v(lambda e: e.tensor_scalar(
                self.L2R[1][0:1, :], self.L2R[1][0:1, :], 1e-5, None, ALU.add))
            lt = P.s(lambda e: e.activation(self.L2R[1][0:1, :], self.L2R[1][0:1, :],
                                            AF.Ln), waits=[vt])
            rt = P.s(lambda e: e.activation(self.L2R[1][0:1, :], self.L2R[1][0:1, :],
                                            AF.Exp, scale=-0.5), waits=[lt])
            mcast = P.v(lambda e: e.tensor_copy(self.M2R[0:1, :], self.L2R[0][0:1, :]),
                        waits=[mt])
            mbt = self.bcast(self.M2R[0:1, :], TC, self.M2B[:, 0:TC], True,
                             [mcast], 0)
            rbt = self.bcast(self.L2R[1][0:1, :], TC, self.R2B[:, 0:TC], False,
                             [rt], 1)
            for dc in range(8):
                oi = (t * 8 + dc) % 2
                s1 = P.v(lambda e, dc=dc, oi=oi: e.tensor_sub(
                    self.OST[oi][:, 0:TC], self.Y2C[:, TC * dc:TC * dc + TC],
                    self.M2B[:, 0:TC]), waits=[mbt, rbt, ost_free[oi]])
                s2 = P.v(lambda e, oi=oi: e.tensor_mul(
                    self.OST[oi][:, 0:TC], self.OST[oi][:, 0:TC],
                    self.R2B[:, 0:TC]), waits=[s1])
                ost_free[oi] = self.P.d(
                    "sync", "dout", lambda e, dc=dc, t=t, oi=oi: e.dma_start(
                        self.out[128 * dc:128 * dc + 128, TC * t:TC * t + TC],
                        self.OST[oi][:, 0:TC]), waits=[s2])
                out_dma.append(ost_free[oi])
        fin = self.P.stream_now("dout")
        P.raw("sync", lambda e, fin=tuple(fin): Prog._waits(e, fin))

    def build(self):
        self.phase0()
        self.phase1()
        self.phase2()
        self.phase3()
        self.phase4()
        self.phase5()
        self.P.replay()
        return self.nc


_HOST_CACHE = {}


def _host_constants():
    if "c" in _HOST_CACHE:
        return _HOST_CACHE["c"]
    inv = 10000.0 ** (-np.arange(0, DHR, 2, dtype=np.float64) / DHR)
    ang = np.arange(S, dtype=np.float64)[:, None] * inv[None, :]
    cos = np.cos(ang).astype(np.float32)
    sin = np.sin(ang).astype(np.float32)
    C32 = np.zeros((32, S), np.float32)
    D32 = np.zeros((32, S), np.float32)
    C32[0::2, :] = cos.T
    C32[1::2, :] = cos.T
    D32[0::2, :] = -sin.T
    D32[1::2, :] = sin.T
    ropeC = np.tile(C32, (4, 1))
    ropeD = np.tile(D32, (4, 1))
    psw = np.zeros((128, 128), np.float32)
    for i in range(128):
        psw[i ^ 1, i] = 1.0
    tri = np.full((128, 512), 80.0, np.float32)
    tri[:, 0:128] = np.where(np.triu(np.ones((128, 128), bool)), 80.0, -60000.0)
    tri = np.tile(tri, (1, 2))
    idm = np.eye(128, dtype=np.float32)
    sel = np.zeros((16, 8 * 128), np.float32)
    for c in range(8):
        sel[2 * c, 128 * c:128 * c + 64] = 1.0
        sel[2 * c + 1, 128 * c + 64:128 * c + 128] = 1.0
    _HOST_CACHE["c"] = (ropeC, ropeD, psw, tri, idm, sel)
    return _HOST_CACHE["c"]


def kernel(L0_attn_x, L0_attn_W_DQ, L0_attn_W_UQ, L0_attn_W_QR, L0_attn_W_DKV,
           L0_attn_W_UK, L0_attn_W_UV, L0_attn_W_KR, L0_attn_W_O,
           L0_ffn_W_fc, L0_ffn_b_fc, L0_ffn_A_fc, L0_ffn_B_fc,
           L0_ffn_W_proj, L0_ffn_b_proj, L0_ffn_A_proj, L0_ffn_B_proj):
    nc = Builder().build()
    ropeC, ropeD, psw, tri, idm, sel = _host_constants()
    f32 = lambda a: np.ascontiguousarray(np.asarray(a, dtype=np.float32))
    shared = {
        "wDQ": f32(L0_attn_W_DQ), "wUQ": f32(L0_attn_W_UQ),
        "wQR": f32(L0_attn_W_QR), "wDKV": f32(L0_attn_W_DKV),
        "wUK": f32(L0_attn_W_UK), "wUV": f32(L0_attn_W_UV),
        "wKR": f32(L0_attn_W_KR),
        "wOr": f32(np.asarray(L0_attn_W_O).reshape(8, 128, 8, 128)
                   .transpose(2, 1, 0, 3).reshape(1024, 1024)),
        "wFCr": f32(np.asarray(L0_ffn_W_fc).T.reshape(8, 128, 64, 64)
                    .transpose(2, 1, 0, 3).reshape(8192, 512)),
        "bFC": f32(np.asarray(L0_ffn_b_fc).reshape(32, 128).T),
        "aFCT": f32(np.asarray(L0_ffn_A_fc).T, ), "bFCbT": f32(np.asarray(L0_ffn_B_fc).T),
        "wPRr": f32(np.asarray(L0_ffn_W_proj).T.reshape(4, 8, 128, 16, 64)
                    .transpose(0, 3, 2, 1, 4).reshape(8192, 512)),
        "bPR": f32(np.asarray(L0_ffn_b_proj).reshape(8, 128).T),
        "aPRT": f32(np.asarray(L0_ffn_A_proj).T),
        "bPRbT": f32(np.asarray(L0_ffn_B_proj).T),
        "ropeC": ropeC, "ropeD": ropeD, "pswap": psw, "trim": tri,
        "ident": idm, "sel2": sel,
    }
    x = np.asarray(L0_attn_x, dtype=np.float32)
    in_maps = []
    for b in range(8):
        m = dict(shared)
        m["xT"] = np.ascontiguousarray(x[b].T)
        in_maps.append(m)
    res = run_bass_kernel_spmd(nc, in_maps, core_ids=list(range(8)))
    outs = [np.ascontiguousarray(np.asarray(r["out"], dtype=np.float32).T)
            for r in res.results]
    return np.stack(outs, axis=0)


# version 25
# speedup vs baseline: 1.2447x; 1.2447x over previous
import math
import numpy as np
import concourse.bass as bass
import concourse.mybir as mybir
from concourse.bass_utils import run_bass_kernel_spmd

F32 = mybir.dt.float32
F16 = mybir.dt.float16
BF16 = mybir.dt.bfloat16
I32 = mybir.dt.int32
I8 = mybir.dt.int8
AF = mybir.ActivationFunctionType
ALU = mybir.AluOpType
AXX = mybir.AxisListType.X

S = 2048
D = 1024
H = 16
DH = 64
DHR = 32
DL = 256
DQK = DH + DHR
FF = 4096
RL = 16
ISQ = 1.0 / math.sqrt(float(DQK))
TC = 256
NTC = S // TC


class Prog:

    def __init__(self, nc):
        self.nc = nc
        self.ops = {"sync": [], "vector": [], "scalar": [], "tensor": [], "gpsimd": []}
        self.ch = {e: nc.alloc_semaphore("ch_" + e)
                   for e in ("vector", "scalar", "gpsimd", "tensor")}
        self.cnt = {"vector": 0, "scalar": 0, "gpsimd": 0, "tensor": 0}
        self.dsem = {}
        self.dexp = {}

    @staticmethod
    def _waits(e, waits):
        best = {}
        for sem, val in waits:
            k = id(sem)
            if k not in best or best[k][1] < val:
                best[k] = (sem, val)
        for sem, val in best.values():
            e.wait_ge(sem, val)

    def _chained(self, eng, fn, waits):
        ch = self.ch[eng]
        w = tuple((s, v) for (s, v) in ((p for p in waits if p is not None))
                  if s is not None and s is not ch)
        n0 = self.cnt[eng]

        def emit(e, fn=fn, w=w, ch=ch, n0=n0):
            Prog._waits(e, w)
            if n0:
                e.wait_ge(ch, n0)
            fn(e).then_inc(ch, 1)
        self.ops[eng].append(emit)
        self.cnt[eng] += 1
        return (ch, self.cnt[eng])

    def v(self, fn, waits=()):
        return self._chained("vector", fn, waits)

    def s(self, fn, waits=()):
        return self._chained("scalar", fn, waits)

    def g(self, fn, waits=()):
        return self._chained("gpsimd", fn, waits)

    def t(self, fn, waits=(), inc=False):
        ch = self.ch["tensor"]
        w = tuple((s, v) for (s, v) in (p for p in waits if p is not None)
                  if s is not None and s is not ch)
        if inc:
            def emit(e, fn=fn, w=w, ch=ch):
                Prog._waits(e, w)
                fn(e).then_inc(ch, 1)
            self.ops["tensor"].append(emit)
            self.cnt["tensor"] += 1
            return (ch, self.cnt["tensor"])

        def emit2(e, fn=fn, w=w):
            Prog._waits(e, w)
            fn(e)
        self.ops["tensor"].append(emit2)
        return None

    NSLOT = 4

    def d(self, queue, semname, fn, waits=()):
        if semname not in self.dsem:
            self.dsem[semname] = [self.nc.alloc_semaphore("d_%s%d" % (semname, i))
                                  for i in range(self.NSLOT)]
            self.dexp[semname] = 0
        i = self.dexp[semname]
        self.dexp[semname] += 1
        sem = self.dsem[semname][i % self.NSLOT]
        tgt = 16 * (i // self.NSLOT + 1)
        w = [(s, v) for (s, v) in (p for p in waits if p is not None)
             if s is not None]
        if tgt > 16:
            w.append((sem, tgt - 16))
        w = tuple(w)

        def emit(e, fn=fn, w=w, sem=sem):
            Prog._waits(e, w)
            fn(e).then_inc(sem, 16)
        self.ops[queue].append(emit)
        return (sem, tgt)

    def stream_now(self, semname):
        if semname not in self.dsem:
            return []
        n = self.dexp[semname]
        toks = []
        for sl in range(self.NSLOT):
            cnt = (n - sl + self.NSLOT - 1) // self.NSLOT
            if cnt > 0:
                toks.append((self.dsem[semname][sl], 16 * cnt))
        return toks

    def raw(self, queue, fn):
        self.ops[queue].append(fn)

    def replay(self):
        with self.nc.Block("main") as blk:
            blk.sync(lambda e: [f(e) for f in self.ops["sync"]])
            blk.vector(lambda e: [f(e) for f in self.ops["vector"]])
            blk.scalar(lambda e: [f(e) for f in self.ops["scalar"]])
            blk.tensor(lambda e: [f(e) for f in self.ops["tensor"]])
            blk.gpsimd(lambda e: [f(e) for f in self.ops["gpsimd"]])


class SbAlloc:
    def __init__(self, nc):
        self.nc = nc
        self.BASE = (nc.sbuf_base + 63) // 64 * 64
        self.LIMIT = nc.sbuf_top
        self.placed = []

    def alloc(self, name, shape, dtype, ph0, ph1, align=64, side=None):
        nb = int(np.prod(shape[1:])) * mybir.dt.size(dtype)
        high = (ph1 >= 5 and ph0 <= 4) or (ph1 - ph0) >= 3
        if side == "high":
            high = True
        if high:
            cands = sorted({self.LIMIT - nb} | {s - nb for (s, e, a, b, n)
                                                in self.placed}, reverse=True)
        else:
            cands = sorted({self.BASE} | {e for (s, e, a, b, n) in self.placed})
        for off in cands:
            off = (off // align * align) if high else ((off + align - 1)
                                                       // align * align)
            if off < self.BASE or off + nb > self.LIMIT:
                continue
            if all((ph1 < a or b < ph0) or (off + nb <= s or e <= off)
                   for (s, e, a, b, n) in self.placed):
                self.placed.append((off, off + nb, ph0, ph1, name))
                return self.nc.alloc_sbuf_tensor_at(name, list(shape), dtype, offset=off)
        raise RuntimeError("SBUF full for %s (%d B)" % (name, nb))


class Builder:
    def __init__(self):
        nc = bass.Bass("TRN2", target_bir_lowering=False)
        self.nc = nc
        self.P = Prog(nc)
        self.dd = {}

        def din(name, shape):
            self.dd[name] = nc.dram_tensor(name, list(shape), F32, kind="ExternalInput")
            return self.dd[name]
        self.xT = din("xT", [D, S])
        self.wDQ = din("wDQ", [D, DL]); self.wUQ = din("wUQ", [DL, D])
        self.wQR = din("wQR", [DL, H * DHR]); self.wDKV = din("wDKV", [D, DL])
        self.wUK = din("wUK", [DL, D]); self.wUV = din("wUV", [DL, D])
        self.wKR = din("wKR", [D, DHR])
        self.wOr = din("wOr", [D, D])
        self.wFCr = din("wFCr", [64 * 128, 512])
        self.wPRr = din("wPRr", [64 * 128, 512])
        self.bFC = din("bFC", [128, FF // 128])
        self.aFCT = din("aFCT", [D, RL]); self.bFCbT = din("bFCbT", [RL, FF])
        self.bPR = din("bPR", [128, D // 128])
        self.aPRT = din("aPRT", [FF, RL]); self.bPRbT = din("bPRbT", [RL, D])
        self.ropeC = din("ropeC", [128, S]); self.ropeD = din("ropeD", [128, S])
        self.pswap = din("pswap", [128, 128]); self.trim = din("trim", [128, 1024])
        self.ident = din("ident", [128, 128]); self.sel2 = din("sel2", [16, 1024])
        self.out = nc.dram_tensor("out", [D, S], F32, kind="ExternalOutput")

        A = SbAlloc(nc)
        self.XA = A.alloc("XA", [128, 8 * S], BF16, 0, 3)
        self.CQ = A.alloc("CQ", [128, 2 * S], BF16, 0, 2)
        self.CKV = A.alloc("CKV", [128, 2 * S], BF16, 0, 2)
        self.QRt = A.alloc("QRt", [128, 4 * S], BF16, 0, 2)
        self.KRt = A.alloc("KRt", [32, S], BF16, 0, 2)
        self.WUQ = A.alloc("WUQ", [128, 2 * D], BF16, 0, 2)
        self.WUK = A.alloc("WUK", [128, 2 * D], BF16, 0, 2)
        self.WQR = A.alloc("WQR", [128, 2 * H * DHR], BF16, 0, 0)
        self.WUV = A.alloc("WUV", [128, 2 * D], BF16, 0, 1)
        self.WDQ = A.alloc("WDQ", [128, 8 * DL], BF16, 0, 0)
        self.WDKV = A.alloc("WDKV", [128, 8 * DL], BF16, 0, 0)
        self.WKR = A.alloc("WKR", [128, 8 * DHR], BF16, 0, 0)
        self.RC = A.alloc("RC", [128, S], BF16, 0, 0)
        self.RDm = A.alloc("RDm", [128, S], BF16, 0, 0)
        self.PSW = A.alloc("PSW", [128, 128], BF16, 0, 0)
        self.STG = A.alloc("STG", [128, 4096], F32, 0, 2)
        self.SC1 = A.alloc("SC1", [128, S], BF16, 0, 0)
        self.SC2 = A.alloc("SC2", [128, S], BF16, 0, 0)
        self.VP = A.alloc("VP", [128, 16 * 1040], BF16, 1, 2)
        self.ODSC = A.alloc("ODSC", [64, S], BF16, 2, 2)
        self.QHB = [A.alloc("QH%d" % i, [96, S], BF16, 1, 2) for i in range(2)]
        self.KHB = [A.alloc("KH%d" % i, [96, S], BF16, 1, 2) for i in range(2)]
        self.CLB = [A.alloc("CL%d" % i, [128, 1024], F16, 2, 2) for i in range(4)]
        self.PTB = [A.alloc("PT%d" % i, [128, 1024], BF16, 2, 2) for i in range(4)]
        self.TRI = A.alloc("TRI", [128, 1024], F32, 0, 2)
        self.DRW = A.alloc("DRW", [128, S], F32, 2, 3, side="high")
        self.DNT = A.alloc("DNT", [16, S], F32, 2, 3, side="high")
        self.DNTB = A.alloc("DNTB", [16, S], BF16, 3, 3, side="high")
        self.MXW = A.alloc("MXW", [128, 64], F32, 2, 7)
        self.MXR = A.alloc("MXR", [128, 2], F32, 2, 7)
        self.MX1 = A.alloc("MX1", [1, 64], F32, 2, 7)
        self.SML = A.alloc("SML", [128, 16], F32, 2, 7)
        self.ONEC = A.alloc("ONEC", [128, 8], BF16, 0, 7)
        self.ONERB = A.alloc("ONERB", [1, 128], BF16, 0, 7)
        self.ONERF = A.alloc("ONERF", [1, 128], F32, 0, 7)
        self.IDT = A.alloc("IDT", [128, 128], BF16, 0, 7)
        self.SELB = A.alloc("SELB", [16, 1024], BF16, 0, 3)
        self.AFT = A.alloc("AFT", [128, 128], BF16, 0, 7)
        self.APT = A.alloc("APT", [128, 512], BF16, 0, 7)
        self.XFM = A.alloc("XFM", [128, 8 * S], F32, 3, 4)
        self.WOBs = A.alloc("WOBs", [128, D], BF16, 3, 3)
        self.WST = A.alloc("WST", [128, 2 * D], F32, 3, 3)
        self.RB2 = A.alloc("RB2", [128, S], BF16, 3, 3)
        self.RTF = A.alloc("RTF", [128, 8 * FF], BF16, 3, 7)
        self.QSG = [A.alloc("QSG%d" % i, [128, 512], F32, 3, 5) for i in range(3)]
        self.QTI = [A.alloc("QTI%d" % i, [128, 512], I8, 3, 5) for i in range(2)]
        self.YC1 = A.alloc("YC1", [128, S], BF16, 4, 4)
        self.STY = A.alloc("STY", [1, S], F32, 4, 4)
        self.STY2 = A.alloc("STY2", [1, S], F32, 4, 4)
        self.MBC = A.alloc("MBC", [128, 1024], F32, 4, 4)
        self.RBC = A.alloc("RBC", [128, 1024], F32, 4, 4)
        self.XT16 = A.alloc("XT16", [128, 8 * S], BF16, 4, 7)
        self.RTP = A.alloc("RTP", [128, 32 * D], BF16, 5, 7)
        self.HH = A.alloc("HH", [128, 32 * TC], BF16, 5, 7)
        self.BSF = A.alloc("BSF", [16, FF], BF16, 5, 7)
        self.Y2C = A.alloc("Y2C", [128, 8 * TC], BF16, 5, 7)
        self.UO = A.alloc("UO", [16, S], BF16, 5, 7, side="high")
        self.BSP = A.alloc("BSP", [16, D], BF16, 5, 7, side="high")
        self.UPt = A.alloc("UPt", [16, TC], BF16, 5, 7, side="high")
        self.BFCS = A.alloc("BFCS", [128, 32], F32, 5, 7, side="high")
        self.BPRS = A.alloc("BPRS", [128, 8], F32, 5, 7, side="high")
        self.PTMP = A.alloc("PTMP", [128, 2 * TC], F32, 5, 7)
        self.TSQ = A.alloc("TSQ", [128, TC], BF16, 7, 7)
        self.OST = [A.alloc("OST%d" % i, [128, TC], F32, 7, 7) for i in range(2)]
        self.L2R = [A.alloc("L2R%d" % i, [1, TC], F32, 7, 7) for i in range(3)]
        self.M2R = A.alloc("M2R", [1, TC], BF16, 7, 7, side="high")
        self.M2B = A.alloc("M2B", [128, TC], BF16, 7, 7)
        self.R2B = A.alloc("R2B", [128, TC], F32, 7, 7)

        self.BIG = nc.alloc_psum_tensor("BIG", [128, 1024], F32)
        self.PSD = [nc.alloc_psum_tensor("PSD%d" % i, [128, 1024], F32)
                    for i in range(3)]
        self.PSB = [self.PSD[i][:, 512 * hh:512 * hh + 512]
                    for i in range(3) for hh in range(2)]
        self.ps_free = [None] * 6
        self.stg_free = [None, None]
        self.stage_i = 0
        xoff = next(s for (s, e, a, b, n) in A.placed if n == "XFM")
        okn = {"CQ", "CKV", "QRt", "KRt", "WUQ", "WUK"}
        self.x2_early_n = 0
        for d in range(8):
            lo, hi = xoff + 8192 * d, xoff + 8192 * (d + 1)
            if all((not (s < hi and lo < e)) or b < 2 or a > 2 or n in okn
                   for (s, e, a, b, n) in A.placed):
                self.x2_early_n = d + 1
            else:
                break

    def stage(self, src_ap, width, cast_fn):
        h = self.stage_i % 2
        self.stage_i += 1
        sl = self.STG[:, 2048 * h:2048 * h + width]
        ld = self.P.d("sync", "din",
                      lambda e, sl=sl, src_ap=src_ap: e.dma_start(sl, src_ap),
                      waits=[self.stg_free[h]] if self.stg_free[h] else [])
        tok = cast_fn(sl, ld)
        self.stg_free[h] = tok
        return tok

    def stage3(self, src_ap3, width, a, cast_fn):
        h = self.stage_i % 2
        self.stage_i += 1
        sl = self.STG[:, 2048 * h:2048 * h + width]
        sl3 = sl.rearrange("p (a r) -> p a r", a=a)
        ld = self.P.d("sync", "din",
                      lambda e, sl3=sl3, src_ap3=src_ap3: e.dma_start(sl3, src_ap3),
                      waits=[self.stg_free[h]] if self.stg_free[h] else [])
        tok = cast_fn(sl, ld)
        self.stg_free[h] = tok
        return tok

    def mmg(self, outap, parts, waits, inc=True):
        tok = None
        n = len(parts)
        for i, (la, ra) in enumerate(parts):
            tok = self.P.t(
                lambda e, la=la, ra=ra, outap=outap, st=(i == 0), sp=(i == n - 1):
                e.matmul(outap, la, ra, start=st, stop=sp),
                waits=(waits if i == 0 else ()), inc=(inc and i == n - 1))
        return tok

    def bcast(self, row_ap, n, dst_ap, dtype_bf16, waits, psl):
        one = self.ONERB if dtype_bf16 else self.ONERF
        mmtok = self.mmg(self.PSB[psl][:, 0:n], [(one[0:1, :], row_ap)],
                         waits + [self.ps_free[psl], self.one_r], inc=True)
        ev = self.P.v(lambda e, dst_ap=dst_ap, psl=psl, n=n: e.tensor_copy(
            dst_ap, self.PSB[psl][:, 0:n]), waits=[mmtok])
        self.ps_free[psl] = ev
        return ev

    def phase0(self):
        P, nc = self.P, self.nc
        xq_free = [None] * 4
        xtok = None
        for c in range(16):
            hq = c % 4
            q, qs = (("sync", "din"), ("scalar", "dis"))[c % 2]
            sl = self.STG[:, 1024 * hq:1024 * hq + 1024]
            ld = P.d(q, qs, lambda e, sl=sl, c=c: e.dma_start(
                sl, self.xT[128 * (c // 2):128 * (c // 2) + 128,
                            1024 * (c % 2):1024 * (c % 2) + 1024]),
                waits=[xq_free[hq]] if xq_free[hq] else [])
            xtok = P.v(lambda e, sl=sl, c=c: e.tensor_copy(
                self.XA[:, 2048 * (c // 2) + 1024 * (c % 2):
                        2048 * (c // 2) + 1024 * (c % 2) + 1024], sl),
                waits=[ld])
            xq_free[hq] = xtok
        self.stg_free = [xtok, xtok]
        self.x_ready = self.stg_free[(self.stage_i - 1) % 2]

        def load_w(dr, K, M, dst):
            tok = None
            for k in range(K // 128):
                tok = self.stage(dr[128 * k:128 * k + 128, :], M,
                                 lambda sl, ld, k=k: P.v(
                                     lambda e, sl=sl, k=k: e.tensor_copy(
                                         dst[:, M * k:M * k + M], sl), waits=[ld]))
            return tok
        self.wdq_r = load_w(self.wDQ, D, DL, self.WDQ)
        self.wdkv_r = load_w(self.wDKV, D, DL, self.WDKV)
        self.wkr_r = load_w(self.wKR, D, DHR, self.WKR)
        self.wqr_r = load_w(self.wQR, DL, H * DHR, self.WQR)
        self.wuq_r = load_w(self.wUQ, DL, D, self.WUQ)
        self.wuk_r = load_w(self.wUK, DL, D, self.WUK)
        self.wuv_r = load_w(self.wUV, DL, D, self.WUV)
        cp = lambda dst: (lambda sl, ld: P.v(
            lambda e, sl=sl, dst=dst: e.tensor_copy(dst, sl), waits=[ld]))
        self.rc_r = self.stage(self.ropeC[:, :], 2048, cp(self.RC[:, :]))
        self.rd_r = self.stage(self.ropeD[:, :], 2048, cp(self.RDm[:, :]))
        self.psw_r = self.stage(self.pswap[:, :], 128, cp(self.PSW[:, :]))
        self.tri_r = self.stage(self.trim[:, :], 1024, cp(self.TRI[:, :]))
        self.idt_r = self.stage(self.ident[:, :], 128, cp(self.IDT[:, :]))
        h_se = self.stage_i % 2
        self.stage_i += 1
        sl_se = self.STG[0:16, 2048 * h_se:2048 * h_se + 1024]
        ld_se = self.P.d("sync", "din",
                         lambda e, sl_se=sl_se: e.dma_start(sl_se, self.sel2[:, :]),
                         waits=[self.stg_free[h_se]] if self.stg_free[h_se] else [])
        self.sel_r = self.P.v(lambda e, sl_se=sl_se: e.tensor_copy(
            self.SELB[:, :], sl_se), waits=[ld_se])
        self.stg_free[h_se] = self.sel_r
        self.aft_r = self.stage3(self.aFCT.rearrange("(a p) r -> p a r", p=128),
                                 128, 8, cp(self.AFT[:, :]))
        self.apt_r = self.stage3(self.aPRT.rearrange("(a p) r -> p a r", p=128),
                                 512, 32, cp(self.APT[:, :]))
        self.one_r = P.v(lambda e: e.memset(self.ONEC[:, :], 1.0))
        self.one_r = P.v(lambda e: e.memset(self.ONERB[0:1, :], 1.0))
        self.one_r = P.v(lambda e: e.memset(self.ONERF[0:1, :], 1.0))

        def c_proj(dstT, wtile, wrd):
            ev = None
            for m in range(2):
                for nb in range(4):
                    sl = (m * 4 + nb) % 2
                    mmtok = self.mmg(
                        self.PSB[sl][:, 0:512],
                        [(wtile[:, DL * i + 128 * m: DL * i + 128 * m + 128],
                          self.XA[:, 2048 * i + 512 * nb: 2048 * i + 512 * nb + 512])
                         for i in range(8)],
                        [self.x_ready, wrd, self.ps_free[sl]])
                    ev = P.v(lambda e, m=m, nb=nb, sl=sl, dstT=dstT: e.tensor_copy(
                        dstT[:, 2048 * m + 512 * nb: 2048 * m + 512 * nb + 512],
                        self.PSB[sl][:, 0:512]), waits=[mmtok])
                    self.ps_free[sl] = ev
            return ev
        self.cq_r = c_proj(self.CQ, self.WDQ, self.wdq_r)
        self.ckv_r = c_proj(self.CKV, self.WDKV, self.wdkv_r)

        def rope(rows, dst_ap, raw_tok, extra=()):
            sw = None
            for nb in range(4):
                sl = 2 + nb % 2
                mmtok = self.mmg(
                    self.PSB[sl][0:rows, 0:512],
                    [(self.PSW[0:rows, 0:rows],
                      self.SC1[0:rows, 512 * nb:512 * nb + 512])],
                    [raw_tok, self.psw_r, self.ps_free[sl]] + list(extra))
                sw = P.v(lambda e, nb=nb, sl=sl, rows=rows: e.tensor_copy(
                    self.SC2[0:rows, 512 * nb:512 * nb + 512],
                    self.PSB[sl][0:rows, 0:512]), waits=[mmtok])
                self.ps_free[sl] = sw
            t1 = P.v(lambda e, rows=rows: e.tensor_mul(
                self.SC2[0:rows, 0:2048], self.SC2[0:rows, 0:2048],
                self.RDm[0:rows, :]), waits=[sw, self.rd_r])
            P.v(lambda e, rows=rows: e.tensor_mul(
                self.SC1[0:rows, 0:2048], self.SC1[0:rows, 0:2048],
                self.RC[0:rows, :]), waits=[self.rc_r])
            t3 = P.v(lambda e, rows=rows, dst_ap=dst_ap: e.tensor_add(
                dst_ap, self.SC1[0:rows, 0:2048], self.SC2[0:rows, 0:2048]),
                waits=[t1])
            return t3

        raw = None
        for nb in range(4):
            sl = nb % 2
            mmtok = self.mmg(
                self.PSB[sl][0:32, 0:512],
                [(self.WKR[:, DHR * i: DHR * i + DHR],
                  self.XA[:, 2048 * i + 512 * nb: 2048 * i + 512 * nb + 512])
                 for i in range(8)],
                [self.x_ready, self.wkr_r, self.ps_free[sl]])
            raw = P.v(lambda e, nb=nb, sl=sl: e.tensor_copy(
                self.SC1[0:32, 512 * nb:512 * nb + 512],
                self.PSB[sl][0:32, 0:512]), waits=[mmtok])
            self.ps_free[sl] = raw
        self.kr_rdy = rope(32, self.KRt[0:32, 0:2048], raw)

        self.qr_rdy = []
        for g in range(4):
            raw = None
            for nb in range(4):
                sl = nb % 2
                mmtok = self.mmg(
                    self.PSB[sl][:, 0:512],
                    [(self.WQR[:, (H * DHR) * l + 128 * g:
                               (H * DHR) * l + 128 * g + 128],
                      self.CQ[:, 2048 * l + 512 * nb: 2048 * l + 512 * nb + 512])
                     for l in range(2)],
                    [self.cq_r, self.wqr_r, self.ps_free[sl]])
                raw = P.v(lambda e, nb=nb, sl=sl: e.tensor_copy(
                    self.SC1[:, 512 * nb:512 * nb + 512], self.PSB[sl][:, 0:512]),
                    waits=[mmtok])
                self.ps_free[sl] = raw
            self.qr_rdy.append(
                rope(128, self.QRt[:, 2048 * g:2048 * g + 2048], raw))
        self.qr_all = self.qr_rdy[-1]

    def phase1(self):
        P = self.P
        self.vp_ms = P.v(lambda e: e.memset(self.VP[:, :], 1.0),
                         waits=[self.qr_rdy[-1]])
        for tc in range(16):
            for hf in range(2):
                sl = 2 + hf
                mmtok = self.mmg(
                    self.PSB[sl][:, 0:512],
                    [(self.CKV[:, 2048 * l + 128 * tc: 2048 * l + 128 * tc + 128],
                      self.WUV[:, D * l + 512 * hf: D * l + 512 * hf + 512])
                     for l in range(2)],
                    [self.ckv_r, self.wuv_r, self.ps_free[sl], self.vp_ms])
                base = 1040 * tc + 520 * hf
                dstv = self.VP[:, base:base + 520].rearrange(
                    "p (q f) -> p q f", q=8)
                srcv = self.PSB[sl][:, 0:512].rearrange("p (q f) -> p q f", q=8)
                ev1 = P.v(lambda e, dstv=dstv, srcv=srcv: e.tensor_copy(
                    dstv[:, :, 0:64], srcv[:, :, 0:64]), waits=[mmtok])
                self.ps_free[sl] = ev1
        self.x_done_t = (self.P.ch["tensor"], self.P.cnt["tensor"])

    def slot_wl(self, g):
        if g >= 3:
            return [self.slot_free.get(g - 3)]
        return [self.ps_free[2 * (g % 3)], self.ps_free[2 * (g % 3) + 1]]

    def emit_production(self, h, guard):
        for step in self.production_steps(h, guard):
            step()

    def production_steps(self, h, guard):
        P = self.P
        b = h % 2
        self.prod[h] = prod = {"kr": self.krk[b]}

        def q_step(nb):
            def go():
                g = self.rot_g
                self.rot_g += 1
                sl = 2 * (g % 3)
                mmtok = self.mmg(
                    self.PSB[sl][0:64, 0:512],
                    [(self.WUQ[:, D * l + DH * h: D * l + DH * h + DH],
                      self.CQ[:, 2048 * l + 512 * nb: 2048 * l + 512 * nb + 512])
                     for l in range(2)],
                    [self.cq_r, self.wuq_r] + self.slot_wl(g) + guard)
                qev = P.s(lambda e, b=b, nb=nb, sl=sl: e.copy(
                    self.QHB[b][0:64, 512 * nb:512 * nb + 512],
                    self.PSB[sl][0:64, 0:512]), waits=[mmtok] + guard)
                self.slot_free[g] = qev
                prod["q"] = qev
            return go

        def k_step(nb):
            def go():
                g = self.rot_g
                self.rot_g += 1
                sl = 2 * (g % 3)
                mmtok = self.mmg(
                    self.PSB[sl][0:64, 0:512],
                    [(self.WUK[:, D * l + DH * h: D * l + DH * h + DH],
                      self.CKV[:, 2048 * l + 512 * nb: 2048 * l + 512 * nb + 512])
                     for l in range(2)],
                    [self.ckv_r, self.wuk_r] + self.slot_wl(g) + guard)
                kev = P.s(lambda e, b=b, nb=nb, sl=sl: e.copy(
                    self.KHB[b][0:64, 512 * nb:512 * nb + 512],
                    self.PSB[sl][0:64, 0:512]), waits=[mmtok] + guard)
                self.slot_free[g] = kev
                prod["k"] = kev
            return go

        def qr_step():
            def go():
                prod["qr"] = P.d("scalar", "dqr", lambda e, h=h, b=b: e.dma_start(
                    self.QHB[b][64:96, 0:2048],
                    self.QRt[32 * (h % 4):32 * (h % 4) + 32,
                             2048 * (h // 4):2048 * (h // 4) + 2048]),
                    waits=[self.qr_all, prod["q"], self.x_done_t] + guard)
            return go

        return ([q_step(nb) for nb in range(4)] + [qr_step()]
                + [k_step(nb) for nb in range(4)])

    def emit_absmax_dma(self, ci):
        P = self.P
        fc = ci < 32
        mat = self.wFCr if fc else self.wPRr
        j = ci % 32
        h4 = ci % 4
        sl = self.STG[:, 1024 * h4:1024 * h4 + 1024]
        sl3 = sl.rearrange("p (b o) -> p b o", b=2)
        ld = P.d("gpsimd", "dax", lambda e, sl3=sl3, mat=mat, j=j:
                 e.dma_start(sl3, mat[256 * j:256 * j + 256, :].rearrange(
                     "(b p) o -> p b o", p=128)),
            waits=[self.stg_free[h4]] if self.stg_free[h4] else [])
        self.ax_ld[ci] = ld

    def emit_absmax_reduce(self, ci):
        P = self.P
        h4 = ci % 4
        sl = self.STG[:, 1024 * h4:1024 * h4 + 1024]
        rd = P.v(lambda e, ci=ci, sl=sl: e.tensor_reduce(
            self.MXW[:, ci:ci + 1], sl, AXX, ALU.max,
            apply_absolute_value=True), waits=[self.ax_ld[ci]])
        self.stg_free[h4] = rd
        self.absmax_last = rd

    def phase2(self):
        P = self.P
        self.prod = {}
        self.last_prod_toks = []
        self.krk = [P.d("scalar", "dqr", lambda e, b=b: e.dma_start(
            self.KHB[b][64:96, 0:2048], self.KRt[0:32, 0:2048]),
            waits=[self.kr_rdy, self.qr_all, self.x_done_t]) for b in range(2)]
        self.rot_g = 0
        self.slot_free = {}
        self.ax_ld = {}
        self.stg_free = [self.stg_free[0], self.stg_free[0],
                         self.stg_free[1], self.stg_free[1]]
        self.emit_production(0, [])
        prod_pending = self.production_steps(1, [])
        for ci in range(4):
            self.emit_absmax_dma(ci)

        clip_pair = {}
        exp_pair = {}
        self.pv_tok = {}
        bigfree = []
        dn_dma = {}
        last_sc = {}
        odsc_free = []
        ti = 0

        def emit_pv(item):
            (i, h, qb, kt, qlo, qext, cs, off, ready, first_of_bank) = item
            pcol = 1040 * kt + 65 * h
            w = [ready]
            if first_of_bank:
                if qb >= 2:
                    w += self.half_free.get((h, qb - 2), [])
                elif h >= 1:
                    w += self.half_free.get((h - 1, qb + 2), [])
            lq = qlo - 512 * qb + 512 * (qb % 2)
            tok = P.t(lambda e, pcol=pcol, lq=lq, qext=qext, cs=cs, off=off,
                      qb=qb, kt=kt: e.matmul(
                          self.BIG[0:65, lq:lq + qext],
                          self.VP[:, pcol:pcol + 65],
                          self.PTB[cs][:, off:off + qext],
                          start=(kt == 0), stop=(kt == 4 * qb + 3)),
                      waits=w, inc=True)
            self.pv_tok[i] = tok
            if kt == 4 * qb + 3:
                emit_bank_evac(item, tok)

        def emit_bank_evac(item, bpv):
            (i, h, qb, kt, qlo, qext, cs, off, ready, fob) = item
            even = (h % 2 == 0)
            hq = 512 * (qb % 2)
            dtok = P.s(lambda e, qb=qb, hq=hq: e.copy(
                self.DRW[64:65, 512 * qb:512 * qb + 512],
                self.BIG[64:65, hq:hq + 512]),
                waits=[bpv, dn_dma.get(h - 1)])
            if even:
                aev = P.v(lambda e, h=h, qb=qb, hq=hq: e.tensor_copy(
                    self.XA[0:64, 2048 * (h // 2) + 512 * qb:
                            2048 * (h // 2) + 512 * qb + 512],
                    self.BIG[0:64, hq:hq + 512]),
                    waits=[bpv, self.x_done_t])
            else:
                aev = P.v(lambda e, qb=qb, hq=hq: e.tensor_copy(
                    self.ODSC[0:64, 512 * qb:512 * qb + 512],
                    self.BIG[0:64, hq:hq + 512]),
                    waits=[bpv, odsc_free[0] if odsc_free else None])
            self.half_free[(h, qb)] = [dtok, aev]
            head_evacs.setdefault(h, []).append((dtok, aev))

        self.half_free = {}
        self.xa_rdy = {}
        head_evacs = {}
        for h in range(16):
            b = h % 2
            even = (h % 2 == 0)
            pw = self.prod[h]
            tiles = [(qb, kt) for qb in range(4) for kt in range(4 * qb + 4)]
            pend = []
            for idx in range(0, len(tiles), 2):
                pj = (ti + idx) // 2
                g = self.rot_g
                self.rot_g += 1
                pslot = g % 3
                cs = pj % 4
                if idx == len(tiles) // 2:
                    self.emit_absmax_reduce(4 * h)
                    self.emit_absmax_reduce(4 * h + 1)
                    if 4 * h + 4 < 64:
                        self.emit_absmax_dma(4 * h + 4)
                    if 4 * h + 5 < 64:
                        self.emit_absmax_dma(4 * h + 5)
                exts = []
                stok = None
                for half in range(2):
                    qb, kt = tiles[idx + half]
                    qlo = max(512 * qb, 128 * kt)
                    qext = 512 * qb + 512 - qlo
                    exts.append((qb, kt, qlo, qext))
                    w = []
                    if idx == 0:
                        w += [pw["q"], pw["k"], pw["qr"], pw["kr"]]
                    if half == 0:
                        w += self.slot_wl(g)
                    stok = P.t(lambda e, pslot=pslot, kt=kt, qlo=qlo,
                               qext=qext, b=b, half=half:
                               e.matmul(
                                   self.PSD[pslot][:, 512 * half:
                                                   512 * half + qext],
                                   self.KHB[b][0:96, 128 * kt:128 * kt + 128],
                                   self.QHB[b][0:96, qlo:qlo + qext],
                                   start=True, stop=True), waits=w, inc=True)
                qw = 512 + exts[1][3]
                if exts[1][1] >= 4 * exts[1][0]:
                    ctok = P.v(lambda e, pslot=pslot, cs=cs, qw=qw:
                               e.scalar_tensor_tensor(
                                   self.CLB[cs][:, 0:qw],
                                   self.PSD[pslot][:, 0:qw], -80.0,
                                   self.TRI[:, 0:qw], ALU.max, ALU.min),
                               waits=[stok, exp_pair.get(pj - 4), self.tri_r])
                else:
                    ctok = P.v(lambda e, pslot=pslot, cs=cs, qw=qw:
                               e.tensor_scalar(
                                   self.CLB[cs][:, 0:qw],
                                   self.PSD[pslot][:, 0:qw],
                                   80.0, -80.0, ALU.min, ALU.max),
                               waits=[stok, exp_pair.get(pj - 4)])
                clip_pair[pj] = ctok
                self.slot_free[g] = ctok
                etok = P.s(lambda e, cs=cs, qw=qw: e.activation(
                    self.PTB[cs][:, 0:qw], self.CLB[cs][:, 0:qw],
                    AF.Exp, scale=ISQ),
                    waits=[ctok, self.pv_tok.get(2 * pj - 7)])
                exp_pair[pj] = etok
                for half in range(2):
                    qb, kt, qlo, qext = exts[half]
                    pend.append((ti + idx + half, h, qb, kt, qlo, qext,
                                 cs, 512 * half, etok, kt == 0))
                while len(pend) > 6:
                    emit_pv(pend.pop(0))
                if prod_pending and idx >= 2:
                    prod_pending.pop(0)()
            while pend:
                emit_pv(pend.pop(0))
            while prod_pending:
                prod_pending.pop(0)()
            last_sc[h] = stok
            ti += len(tiles)

            aev_l = [a for (d, a) in head_evacs.get(h, [])]
            if not even:
                od = P.d("sync", "dod", lambda e, h=h: e.dma_start(
                    self.XA[64:128, 2048 * (h // 2):2048 * (h // 2) + 2048],
                    self.ODSC[0:64, 0:2048]), waits=aev_l + [self.x_done_t])
                odsc_free[:] = [od]
                self.xa_rdy.setdefault(h // 2, []).append(od)
            else:
                self.xa_rdy.setdefault(h // 2, []).append(aev_l[-1])
            dn_dma[h] = P.d("sync", "ddn", lambda e, h=h: e.dma_start(
                self.DNT[h:h + 1, 0:2048], self.DRW[64:65, 0:2048]),
                waits=[d for (d, a) in head_evacs.get(h, [])])
            bigfree = [t for pair in head_evacs.get(h, []) for t in pair]
            if h + 2 <= 15:
                prod_pending = self.production_steps(h + 2, [last_sc[h]])
            if h == 14:
                early = [self.prod[15]["q"], self.prod[15]["k"]] \
                    + P.stream_now("dqr")
                self.x2_tok = []
                for d in range(self.x2_early_n):
                    q = "scalar" if d % 2 == 0 else "gpsimd"
                    self.x2_tok.append(P.d(q, "dx2", lambda e, d=d: e.dma_start(
                        self.XFM[:, 2048 * d:2048 * d + 2048],
                        self.xT[128 * d:128 * d + 128, :]), waits=early))
            self.emit_absmax_reduce(4 * h + 2)
            self.emit_absmax_reduce(4 * h + 3)
            if 4 * h + 6 < 64:
                self.emit_absmax_dma(4 * h + 6)
            if 4 * h + 7 < 64:
                self.emit_absmax_dma(4 * h + 7)
        self.dn_last = dn_dma[15]
        self.attn_evac_last = list(bigfree)
        self.last_sc = last_sc
        for j in range(3):
            g = self.rot_g - 1 - j
            self.ps_free[2 * (g % 3)] = self.slot_free[g]
            self.ps_free[2 * (g % 3) + 1] = self.slot_free[g]

    def quant_setup(self):
        self.rtf_v = self.RTF[:, :].rearrange("p (i o) -> p i o", i=8)
        self.rtp_v = self.RTP[:, :].rearrange("p (f o) -> p f o", f=32)
        self.quant_sched = []
        self.qtok_fc = [None] * 32
        self.qtok_pr = {}
        for oc in range(32):
            for half in range(2):
                col = 128 * oc + 64 * half
                cfc = 2 * oc + half
                self.quant_sched.append(
                    ("fc", oc, self.wFCr[128 * cfc:128 * cfc + 128, :],
                     self.rtf_v[:, :, col:col + 64], 0))
        for dc in range(8):
            for r in range(4):
                for half in range(2):
                    col = 128 * dc + 64 * half
                    cpr = 16 * r + 2 * dc + half
                    self.quant_sched.append(
                        ("pr", dc, self.wPRr[128 * cpr:128 * cpr + 128, :],
                         self.rtp_v[:, 8 * r:8 * r + 8, col:col + 64], 3))
        self.qn = 0
        self.q_prev = {}
        self.q_inflight = []
        self.half_r = self.P.v(lambda e: e.memset(self.SML[:, 6:7], 0.5))

    def _quant_front(self):
        P = self.P
        kind, idx, src_ap, dst_view, smoff = self.quant_sched.pop(0)
        assert kind == "fc" or self.q_rtp_free, "proj quant before RTP region free"
        c = self.qn
        self.qn += 1
        qsg = self.QSG[c % 3]
        qti = self.QTI[c % 2]
        qsg3 = qsg[:, 0:512].rearrange("p (a o) -> p a o", a=8)
        w_dma = [self.q_prev.get(("p4", c - 3))]
        if c < 4:
            w_dma += self.q_rt_free + self.attn_evac_last
        qq, qs = ((("gpsimd", "dqw"), ("scalar", "dqs"))[c % 2] if c < 32
                  else (("gpsimd", "dqw"), ("scalar", "dqs"),
                        ("sync", "dqy"))[c % 3])
        ld = P.d(qq, qs,
                 lambda e, qsg=qsg, src_ap=src_ap: e.dma_start(
                     qsg[:, 0:512], src_ap),
                 waits=w_dma)
        p1 = P.s(lambda e, qsg=qsg, smoff=smoff: e.activation(
            qsg[:, 0:512], qsg[:, 0:512], AF.Identity,
            bias=self.SML[:, 6:7], scale=self.SML[:, smoff:smoff + 1]),
            waits=[ld, self.sml_tok, self.half_r])
        p2 = P.s(lambda e, qsg=qsg, qti=qti: e.copy(
            qti[:, 0:512], qsg[:, 0:512]),
            waits=[p1, self.q_prev.get(("p4", c - 2))])
        self.q_prev[("p2", c)] = p2
        self.q_inflight.append((c, kind, idx, qsg, qti, qsg3, dst_view, smoff))

    def _quant_back(self):
        P = self.P
        (c, kind, idx, qsg, qti, qsg3, dst_view, smoff) = self.q_inflight.pop(0)
        p3 = P.v(lambda e, qsg=qsg, qti=qti: e.tensor_tensor(
            qsg[:, 0:512], qsg[:, 0:512], qti[:, 0:512], ALU.is_lt),
            waits=[self.q_prev[("p2", c)]])
        w4 = [p3] + self.q_rt_free
        if kind == "pr":
            w4 += self.q_rtp_free
        qti3 = qti[:, 0:512].rearrange("p (a o) -> p a o", a=8)
        p4 = P.v(lambda e, qsg3=qsg3, qti3=qti3, dst_view=dst_view:
                 e.tensor_tensor(dst_view, qti3, qsg3, ALU.subtract),
                 waits=w4)
        self.q_prev[("p4", c)] = p4
        if kind == "fc":
            self.qtok_fc[idx] = p4
        else:
            self.qtok_pr[idx] = p4

    def emit_quant_chunk(self, allow_pr=True):
        if len(self.q_inflight) >= 2 or (not self.quant_sched and self.q_inflight):
            self._quant_back()
        if self.quant_sched and (allow_pr or self.quant_sched[0][0] == "fc"):
            self._quant_front()
    

    def phase3(self):
        P = self.P
        mr0 = P.v(lambda e: e.tensor_reduce(
            self.MXR[:, 0:1], self.MXW[:, 0:32], AXX, ALU.max),
            waits=[self.absmax_last])
        mr1 = P.v(lambda e: e.tensor_reduce(
            self.MXR[:, 1:2], self.MXW[:, 32:64], AXX, ALU.max), waits=[])
        mrd = P.d("sync", "dsm", lambda e: e.dma_start(
            self.DRW[0:1, 1792:2048], self.MXR[:, 0:2]), waits=[mr0, mr1])
        gtok = None
        for m in range(2):
            base = 8 * m
            P.v(lambda e, m=m, base=base: e.tensor_reduce(
                self.MX1[0:1, base:base + 1],
                self.DRW[0:1, 1792:2048].rearrange("a (p b) -> a p b", b=2)
                [:, :, m].rearrange("a p -> a p"),
                AXX, ALU.max), waits=[mrd])
            P.v(lambda e, base=base: e.reciprocal(
                self.MX1[0:1, base + 1:base + 2], self.MX1[0:1, base:base + 1]))
            P.v(lambda e, base=base: e.tensor_mul(
                self.MX1[0:1, base + 2:base + 3], self.MX1[0:1, base:base + 1],
                self.MX1[0:1, base + 1:base + 2]))
            P.v(lambda e, base=base: e.tensor_scalar(
                self.MX1[0:1, base + 2:base + 3], self.MX1[0:1, base + 2:base + 3],
                -1.0, 2.0, ALU.mult, ALU.add))
            P.v(lambda e, base=base: e.tensor_mul(
                self.MX1[0:1, base + 1:base + 2], self.MX1[0:1, base + 1:base + 2],
                self.MX1[0:1, base + 2:base + 3]))
            P.v(lambda e, base=base, m=m: e.tensor_scalar(
                self.MX1[0:1, 32 + 3 * m:33 + 3 * m],
                self.MX1[0:1, base + 1:base + 2], 127.0, None, ALU.mult))
            P.v(lambda e, base=base, m=m: e.tensor_scalar(
                self.MX1[0:1, 33 + 3 * m:34 + 3 * m],
                self.MX1[0:1, base + 1:base + 2], 254.0, None, ALU.mult))
            gtok = P.v(lambda e, base=base, m=m: e.tensor_scalar(
                self.MX1[0:1, 34 + 3 * m:35 + 3 * m],
                self.MX1[0:1, base:base + 1], 1.0 / 127.0, None, ALU.mult))
        self.sml_tok = self.bcast(self.MX1[0:1, 32:38], 6, self.SML[:, 0:6],
                                  False, [gtok], 0)

        ln1 = P.s(lambda e: e.activation(self.DNT[:, :], self.DNT[:, :], AF.Ln),
                  waits=[self.dn_last])
        ex1 = P.s(lambda e: e.activation(self.DNT[:, :], self.DNT[:, :], AF.Exp,
                                         scale=-1.0), waits=[ln1])
        dcb = P.v(lambda e: e.tensor_copy(self.DNTB[:, :], self.DNT[:, :]),
                  waits=[ex1])
        self.norm_tok = {}
        for c in range(8):
            rb = self.RB2[:, 0:2048]
            evs = []
            for nb in range(4):
                sl = 2 + nb % 2
                mmtok = self.mmg(
                    self.PSB[sl][:, 0:512],
                    [(self.SELB[0:16, 128 * c:128 * c + 128],
                      self.DNTB[0:16, 512 * nb:512 * nb + 512])],
                    [dcb, self.sel_r, self.ps_free[sl],
                     self.norm_tok.get(c - 1)])
                ev = P.v(lambda e, rb=rb, nb=nb, sl=sl: e.tensor_copy(
                    rb[:, 512 * nb:512 * nb + 512], self.PSB[sl][:, 0:512]),
                    waits=[mmtok])
                self.ps_free[sl] = ev
                evs.append(ev)
            self.norm_tok[c] = P.v(lambda e, c=c, rb=rb: e.tensor_mul(
                self.XA[:, 2048 * c:2048 * c + 2048],
                self.XA[:, 2048 * c:2048 * c + 2048], rb),
                waits=evs + self.xa_rdy[c] + self.attn_evac_last)

        attn_done = (self.P.stream_now("dqr") + self.P.stream_now("dod")
                     + self.P.stream_now("ddn")
                     + [self.last_sc[15], self.absmax_last])
        x2_tok = self.x2_tok
        for d in range(self.x2_early_n, 8):
            q = "scalar" if d % 2 == 0 else "gpsimd"
            x2_tok.append(P.d(q, "dx2", lambda e, d=d: e.dma_start(
                self.XFM[:, 2048 * d:2048 * d + 2048],
                self.xT[128 * d:128 * d + 128, :]), waits=attn_done))
        self.quant_setup()
        self.q_rt_free = attn_done
        self.q_rtp_free = []
        self.y_tok = {}
        wo_cast_prev = {}
        wo_mm_last = {}
        for dslice in range(8):
            wh = dslice % 2
            ldw = P.d("sync", "dws", lambda e, dslice=dslice, wh=wh: e.dma_start(
                self.WST[:, 1024 * wh:1024 * wh + 1024],
                self.wOr[128 * dslice:128 * dslice + 128, :]),
                waits=[wo_cast_prev.get(dslice - 2)]
                      + (attn_done if dslice < 2 else []))
            wo_cast = P.s(lambda e, wh=wh: e.copy(
                self.WOBs[:, :],
                self.WST[:, 1024 * wh:1024 * wh + 1024]),
                waits=[ldw, wo_mm_last.get(dslice - 1),
                       self.absmax_last if dslice == 0 else None])
            wo_cast_prev[dslice] = wo_cast
            for nb in range(4):
                sl = (dslice * 4 + nb) % 4
                wmm = [wo_cast, self.ps_free[sl]]
                if dslice == 0 and nb == 0:
                    wmm += list(self.norm_tok.values())
                mmtok = self.mmg(
                    self.PSB[sl][:, 0:512],
                    [(self.WOBs[:, 128 * hc:128 * hc + 128],
                      self.XA[:, 2048 * hc + 512 * nb: 2048 * hc + 512 * nb + 512])
                     for hc in range(8)],
                    wmm)
                wo_mm_last[dslice] = mmtok
                ytok = P.v(lambda e, dslice=dslice, nb=nb, sl=sl: e.tensor_add(
                    self.XFM[:, 2048 * dslice + 512 * nb:
                             2048 * dslice + 512 * nb + 512],
                    self.PSB[sl][:, 0:512],
                    self.XFM[:, 2048 * dslice + 512 * nb:
                             2048 * dslice + 512 * nb + 512]),
                    waits=[mmtok, x2_tok[dslice]])
                self.ps_free[sl] = ytok
                self.y_tok[(dslice, nb)] = ytok
                self.emit_quant_chunk()
        self.wo_done = (self.P.ch["tensor"], self.P.cnt["tensor"])

    def phase4(self):
        P = self.P
        sy_last = None
        sq_last = None
        for dslice in range(8):
            w = [self.y_tok[(dslice, nb)] for nb in range(4)]
            if sq_last:
                w.append(sq_last)
            cast_tok = P.v(lambda e, dslice=dslice: e.tensor_copy(
                self.YC1[:, :], self.XFM[:, 2048 * dslice:2048 * dslice + 2048]),
                waits=w)
            sq_t = [self.PSB[4][0:1, 0:512], self.PSB[5][0:1, 0:512],
                    self.BIG[0:1, 0:512], self.BIG[0:1, 512:1024]]
            for nb in range(4):
                wmm = [cast_tok, self.one_r]
                if dslice == 0:
                    wmm.append(self.ps_free[nb])
                sy_last = P.t(lambda e, nb=nb, st=(dslice == 0), sp=(dslice == 7):
                              e.matmul(self.PSB[nb][0:1, 0:512],
                                       self.ONEC[:, 0:1],
                                       self.YC1[:, 512 * nb:512 * nb + 512],
                                       start=st, stop=sp),
                              waits=wmm, inc=True)
            sqc = P.s(lambda e: e.activation(self.YC1[:, :], self.YC1[:, :],
                                             AF.Square), waits=[cast_tok, sy_last])
            self.emit_quant_chunk(allow_pr=False)
            self.emit_quant_chunk(allow_pr=False)
            for nb in range(4):
                wmm = [sqc]
                if dslice == 0:
                    wmm += self.attn_evac_last
                    if nb < 2:
                        wmm.append(self.ps_free[4 + nb])
                sq_last = P.t(lambda e, nb=nb, st=(dslice == 0), sp=(dslice == 7),
                              tgt=sq_t[nb]:
                              e.matmul(tgt, self.ONEC[:, 0:1],
                                       self.YC1[:, 512 * nb:512 * nb + 512],
                                       start=st, stop=sp),
                              waits=wmm, inc=True)
        ev1 = None
        for nb in range(4):
            ev1 = P.s(lambda e, nb=nb: e.copy(
                self.STY[0:1, 512 * nb:512 * nb + 512], self.PSB[nb][0:1, 0:512]),
                waits=[sy_last])
            self.ps_free[nb] = ev1
        ev2 = None
        for nb in range(4):
            ev2 = P.s(lambda e, nb=nb, tgt=sq_t[nb]: e.copy(
                self.STY2[0:1, 512 * nb:512 * nb + 512], tgt),
                waits=[sq_last])
        self.big_free_ln1 = ev2
        mtok = P.v(lambda e: e.tensor_scalar(
            self.STY[0:1, :], self.STY[0:1, :], 1.0 / 1024.0, None, ALU.mult),
            waits=[ev1])
        P.v(lambda e: e.tensor_scalar(
            self.STY2[0:1, :], self.STY2[0:1, :], 1.0 / 1024.0, None, ALU.mult),
            waits=[ev2])
        P.v(lambda e: e.tensor_mul(self.STY[0:1, 0:S].broadcast_to((1, S)) if False
                                   else self.YC1[0:1, 0:S].bitcast(F32) if False
                                   else self.MBC[0:1, 0:1024], self.STY[0:1, 0:1024],
                                   self.STY[0:1, 0:1024]))
        P.v(lambda e: e.tensor_sub(self.STY2[0:1, 0:1024], self.STY2[0:1, 0:1024],
                                   self.MBC[0:1, 0:1024]))
        P.v(lambda e: e.tensor_mul(self.MBC[0:1, 0:1024], self.STY[0:1, 1024:2048],
                                   self.STY[0:1, 1024:2048]))
        P.v(lambda e: e.tensor_sub(self.STY2[0:1, 1024:2048],
                                   self.STY2[0:1, 1024:2048],
                                   self.MBC[0:1, 0:1024]))
        vtok = P.v(lambda e: e.tensor_scalar(
            self.STY2[0:1, :], self.STY2[0:1, :], 1e-5, None, ALU.add))
        lt = P.s(lambda e: e.activation(self.STY2[0:1, :], self.STY2[0:1, :],
                                        AF.Ln), waits=[vtok])
        rt = P.s(lambda e: e.activation(self.STY2[0:1, :], self.STY2[0:1, :],
                                        AF.Exp, scale=-0.5), waits=[lt])
        self.xt_rdy = {}
        apl = {}
        for nb in range(4):
            hb = nb % 2
            mb = self.bcast(self.STY[0:1, 512 * nb:512 * nb + 512], 512,
                            self.MBC[:, 512 * hb:512 * hb + 512], False,
                            [mtok, apl.get((nb - 2, 7))], 2 + nb % 2)
            rb = self.bcast(self.STY2[0:1, 512 * nb:512 * nb + 512], 512,
                            self.RBC[:, 512 * hb:512 * hb + 512], False,
                            [rt, apl.get((nb - 2, 7))], 2 + nb % 2)
            for dslice in range(8):
                P.v(lambda e, dslice=dslice, nb=nb, hb=hb: e.tensor_sub(
                    self.XFM[:, 2048 * dslice + 512 * nb:
                             2048 * dslice + 512 * nb + 512],
                    self.XFM[:, 2048 * dslice + 512 * nb:
                             2048 * dslice + 512 * nb + 512],
                    self.MBC[:, 512 * hb:512 * hb + 512]), waits=[mb, rb])
                apl[(nb, dslice)] = P.v(
                    lambda e, dslice=dslice, nb=nb, hb=hb: e.tensor_mul(
                        self.XT16[:, 2048 * dslice + 512 * nb:
                                 2048 * dslice + 512 * nb + 512],
                        self.XFM[:, 2048 * dslice + 512 * nb:
                                 2048 * dslice + 512 * nb + 512],
                        self.RBC[:, 512 * hb:512 * hb + 512]), waits=[])
        for dslice in range(8):
            self.xt_rdy[dslice] = apl[(3, dslice)]
        self.ph4_bar = [self.xt_rdy[7], rt, sq_last, self.absmax_last]
        self.q_rtp_free = list(self.xt_rdy.values()) + [self.wo_done]

    def phase5(self):
        P = self.P
        xt_all = list(self.xt_rdy.values())
        bar = list(self.ph4_bar)
        self.q_rtp_free = xt_all + [self.wo_done]
        while self.q_inflight:
            self._quant_back()
        bfcs_ld = P.d("sync", "dfb", lambda e: e.dma_start(
            self.BFCS[:, :], self.bFC[:, :]), waits=xt_all + bar)
        bprs_ld = P.d("sync", "dfb", lambda e: e.dma_start(
            self.BPRS[:, :], self.bPR[:, :]), waits=xt_all + bar)
        bsf_tok = bsp_tok = None
        for c in range(10):
            pb = self.qn % 3
            self.qn += 1
            qsg = self.QSG[pb]
            src_ap = (self.bFCbT[:, 512 * c:512 * c + 512] if c < 8
                      else self.bPRbT[:, 512 * (c - 8):512 * (c - 8) + 512])
            dst_ap = (self.BSF[:, 512 * c:512 * c + 512] if c < 8
                      else self.BSP[:, 512 * (c - 8):512 * (c - 8) + 512])
            ld = P.d("sync", "dfb", lambda e, qsg=qsg, src_ap=src_ap: e.dma_start(
                qsg[0:16, 0:512], src_ap),
                waits=[self.q_prev.get(("p4", self.qn - 3))] + xt_all + bar)
            tok = P.v(lambda e, qsg=qsg, dst_ap=dst_ap: e.tensor_copy(
                dst_ap, qsg[0:16, 0:512]), waits=[ld] + bar)
            self.q_prev[("p4", self.qn - 1)] = tok
            if c < 8:
                bsf_tok = tok
            else:
                bsp_tok = tok

        ufc_tok = None
        for nb in range(4):
            sl = nb % 2
            mmtok = self.mmg(
                self.PSB[sl][0:16, 0:512],
                [(self.AFT[:, 16 * i:16 * i + 16],
                  self.XT16[:, 2048 * i + 512 * nb: 2048 * i + 512 * nb + 512])
                 for i in range(8)],
                [self.aft_r, self.ps_free[sl]] + xt_all + bar)
            ufc_tok = P.v(lambda e, nb=nb, sl=sl: e.tensor_scalar(
                self.UO[0:16, 512 * nb:512 * nb + 512], self.PSB[sl][0:16, 0:512],
                self.SML[0:16, 1:2], None, ALU.mult), waits=[mmtok, self.sml_tok])
            self.ps_free[sl] = ufc_tok

        for _ in range(24):
            if self.quant_sched or self.q_inflight:
                self.emit_quant_chunk()

        rtf_v = self.rtf_v
        rtp_v = self.rtp_v
        qtok_fc = self.qtok_fc
        qtok_pr = self.qtok_pr
        bar = list(self.ph4_bar)
        ost_free = [None, None]
        hh_tok = {}
        y2_tok = {}
        out_dma = []
        sqmm = None
        e1_prev = e2_prev = None
        for t in range(NTC):
            tbar = bar if t == 0 else []
            for oc in range(32):
                sl = (0, 1, 4, 5)[oc % 4]
                parts = [(rtf_v[:, i, 128 * oc:128 * oc + 128],
                          self.XT16[:, 2048 * i + TC * t: 2048 * i + TC * t + TC])
                         for i in range(8)]
                parts.append((self.BSF[:, 128 * oc:128 * oc + 128],
                              self.UO[0:16, TC * t:TC * t + TC]))
                wfc = [qtok_fc[oc], bsf_tok, ufc_tok,
                       self.ps_free[sl]] + xt_all + tbar
                if t == 0 and oc in (2, 3):
                    wfc.append(self.big_free_ln1)
                mmtok = self.mmg(self.PSB[sl][:, 0:TC], parts, wfc)
                hh_tok[oc] = P.s(lambda e, oc=oc, sl=sl: e.activation(
                    self.HH[:, TC * oc:TC * oc + TC], self.PSB[sl][:, 0:TC],
                    AF.Gelu, bias=self.BFCS[:, oc:oc + 1],
                    scale=self.SML[:, 2:3]),
                    waits=[mmtok, bfcs_ld, self.sml_tok] + tbar)
                self.ps_free[sl] = hh_tok[oc]
                if self.quant_sched or self.q_inflight:
                    self.emit_quant_chunk()
                    self.emit_quant_chunk()
            while self.quant_sched or self.q_inflight:
                self.emit_quant_chunk()
            sl = 2
            mmtok = self.mmg(
                self.PSB[sl][0:16, 0:TC],
                [(self.APT[:, 16 * f:16 * f + 16], self.HH[:, TC * f:TC * f + TC])
                 for f in range(32)],
                [self.apt_r, self.ps_free[sl]] + [hh_tok[oc] for oc in range(32)])
            up_tok = P.v(lambda e, sl=sl: e.tensor_scalar(
                self.UPt[0:16, 0:TC], self.PSB[sl][0:16, 0:TC],
                self.SML[0:16, 4:5], None, ALU.mult), waits=[mmtok])
            self.ps_free[sl] = up_tok
            for dc in range(8):
                sl = (2, 3, 0, 1)[dc % 4]
                parts = [(rtp_v[:, f, 128 * dc:128 * dc + 128],
                          self.HH[:, TC * f:TC * f + TC]) for f in range(32)]
                parts.append((self.BSP[:, 128 * dc:128 * dc + 128],
                              self.UPt[0:16, 0:TC]))
                mmtok = self.mmg(self.PSB[sl][:, 0:TC], parts,
                                 [qtok_pr[dc], bsp_tok, up_tok,
                                  self.ps_free[sl]])
                ph = dc % 2
                pa = P.s(lambda e, dc=dc, sl=sl, ph=ph: e.activation(
                    self.PTMP[:, TC * ph:TC * ph + TC], self.PSB[sl][:, 0:TC],
                    AF.Identity,
                    bias=self.BPRS[:, dc:dc + 1], scale=self.SML[:, 5:6]),
                    waits=[mmtok, bprs_ld, self.sml_tok,
                           y2_tok.get((dc - 2) % 8)] + tbar)
                y2_tok[dc] = P.v(lambda e, dc=dc, t=t, ph=ph: e.tensor_add(
                    self.Y2C[:, TC * dc:TC * dc + TC],
                    self.PTMP[:, TC * ph:TC * ph + TC],
                    self.XT16[:, 2048 * dc + TC * t: 2048 * dc + TC * t + TC]),
                    waits=[pa, e2_prev] + tbar)
                self.ps_free[sl] = y2_tok[dc]
            symm = None
            for dc in range(8):
                wsy = [y2_tok[dc], self.one_r]
                if dc == 0:
                    wsy.append(e1_prev if t else self.big_free_ln1)
                symm = P.t(lambda e, dc=dc, st=(dc == 0), sp=(dc == 7): e.matmul(
                    self.BIG[0:1, 0:TC], self.ONEC[:, 0:1],
                    self.Y2C[:, TC * dc:TC * dc + TC], start=st, stop=sp),
                    waits=wsy, inc=(dc == 7))
                sq = P.s(lambda e, dc=dc: e.activation(
                    self.TSQ[:, 0:TC], self.Y2C[:, TC * dc:TC * dc + TC],
                    AF.Square), waits=[y2_tok[dc], sqmm])
                wsq = [sq]
                if dc == 0:
                    wsq.append(e2_prev if t else self.big_free_ln1)
                sqmm = P.t(lambda e, dc=dc, st=(dc == 0), sp=(dc == 7): e.matmul(
                    self.BIG[0:1, 512:512 + TC], self.ONEC[:, 0:1],
                    self.TSQ[:, 0:TC], start=st, stop=sp),
                    waits=wsq, inc=True)
            e1 = P.s(lambda e: e.copy(self.L2R[0][0:1, :], self.BIG[0:1, 0:TC]),
                     waits=[symm])
            e2 = P.s(lambda e: e.copy(self.L2R[1][0:1, :], self.BIG[0:1, 512:512 + TC]),
                     waits=[sqmm])
            e1_prev, e2_prev = e1, e2
            mt = P.v(lambda e: e.tensor_scalar(
                self.L2R[0][0:1, :], self.L2R[0][0:1, :], 1.0 / 1024.0, None,
                ALU.mult), waits=[e1])
            P.v(lambda e: e.tensor_scalar(
                self.L2R[1][0:1, :], self.L2R[1][0:1, :], 1.0 / 1024.0, None,
                ALU.mult), waits=[e2])
            P.v(lambda e: e.tensor_mul(self.L2R[2][0:1, :], self.L2R[0][0:1, :],
                                       self.L2R[0][0:1, :]))
            P.v(lambda e: e.tensor_sub(self.L2R[1][0:1, :], self.L2R[1][0:1, :],
                                       self.L2R[2][0:1, :]))
            vt = P.v(lambda e: e.tensor_scalar(
                self.L2R[1][0:1, :], self.L2R[1][0:1, :], 1e-5, None, ALU.add))
            lt = P.s(lambda e: e.activation(self.L2R[1][0:1, :], self.L2R[1][0:1, :],
                                            AF.Ln), waits=[vt])
            rt = P.s(lambda e: e.activation(self.L2R[1][0:1, :], self.L2R[1][0:1, :],
                                            AF.Exp, scale=-0.5), waits=[lt])
            mcast = P.v(lambda e: e.tensor_copy(self.M2R[0:1, :], self.L2R[0][0:1, :]),
                        waits=[mt])
            mbt = self.bcast(self.M2R[0:1, :], TC, self.M2B[:, 0:TC], True,
                             [mcast], 0)
            rbt = self.bcast(self.L2R[1][0:1, :], TC, self.R2B[:, 0:TC], False,
                             [rt], 1)
            for dc in range(8):
                oi = (t * 8 + dc) % 2
                s1 = P.v(lambda e, dc=dc, oi=oi: e.tensor_sub(
                    self.OST[oi][:, 0:TC], self.Y2C[:, TC * dc:TC * dc + TC],
                    self.M2B[:, 0:TC]), waits=[mbt, rbt, ost_free[oi]])
                s2 = P.v(lambda e, oi=oi: e.tensor_mul(
                    self.OST[oi][:, 0:TC], self.OST[oi][:, 0:TC],
                    self.R2B[:, 0:TC]), waits=[s1])
                ost_free[oi] = self.P.d(
                    "sync", "dout", lambda e, dc=dc, t=t, oi=oi: e.dma_start(
                        self.out[128 * dc:128 * dc + 128, TC * t:TC * t + TC],
                        self.OST[oi][:, 0:TC]), waits=[s2])
                out_dma.append(ost_free[oi])
        fin = self.P.stream_now("dout")
        P.raw("sync", lambda e, fin=tuple(fin): Prog._waits(e, fin))

    def build(self):
        self.phase0()
        self.phase1()
        self.phase2()
        self.phase3()
        self.phase4()
        self.phase5()
        self.P.replay()
        return self.nc


_HOST_CACHE = {}


def _host_constants():
    if "c" in _HOST_CACHE:
        return _HOST_CACHE["c"]
    inv = 10000.0 ** (-np.arange(0, DHR, 2, dtype=np.float64) / DHR)
    ang = np.arange(S, dtype=np.float64)[:, None] * inv[None, :]
    cos = np.cos(ang).astype(np.float32)
    sin = np.sin(ang).astype(np.float32)
    C32 = np.zeros((32, S), np.float32)
    D32 = np.zeros((32, S), np.float32)
    C32[0::2, :] = cos.T
    C32[1::2, :] = cos.T
    D32[0::2, :] = -sin.T
    D32[1::2, :] = sin.T
    ropeC = np.tile(C32, (4, 1))
    ropeD = np.tile(D32, (4, 1))
    psw = np.zeros((128, 128), np.float32)
    for i in range(128):
        psw[i ^ 1, i] = 1.0
    tri = np.full((128, 512), 80.0, np.float32)
    tri[:, 0:128] = np.where(np.triu(np.ones((128, 128), bool)), 80.0, -60000.0)
    tri = np.tile(tri, (1, 2))
    idm = np.eye(128, dtype=np.float32)
    sel = np.zeros((16, 8 * 128), np.float32)
    for c in range(8):
        sel[2 * c, 128 * c:128 * c + 64] = 1.0
        sel[2 * c + 1, 128 * c + 64:128 * c + 128] = 1.0
    _HOST_CACHE["c"] = (ropeC, ropeD, psw, tri, idm, sel)
    return _HOST_CACHE["c"]


def kernel(L0_attn_x, L0_attn_W_DQ, L0_attn_W_UQ, L0_attn_W_QR, L0_attn_W_DKV,
           L0_attn_W_UK, L0_attn_W_UV, L0_attn_W_KR, L0_attn_W_O,
           L0_ffn_W_fc, L0_ffn_b_fc, L0_ffn_A_fc, L0_ffn_B_fc,
           L0_ffn_W_proj, L0_ffn_b_proj, L0_ffn_A_proj, L0_ffn_B_proj):
    nc = Builder().build()
    ropeC, ropeD, psw, tri, idm, sel = _host_constants()
    f32 = lambda a: np.ascontiguousarray(np.asarray(a, dtype=np.float32))
    shared = {
        "wDQ": f32(L0_attn_W_DQ), "wUQ": f32(L0_attn_W_UQ),
        "wQR": f32(L0_attn_W_QR), "wDKV": f32(L0_attn_W_DKV),
        "wUK": f32(L0_attn_W_UK), "wUV": f32(L0_attn_W_UV),
        "wKR": f32(L0_attn_W_KR),
        "wOr": f32(np.asarray(L0_attn_W_O).reshape(8, 128, 8, 128)
                   .transpose(2, 1, 0, 3).reshape(1024, 1024)),
        "wFCr": f32(np.asarray(L0_ffn_W_fc).T.reshape(8, 128, 64, 64)
                    .transpose(2, 1, 0, 3).reshape(8192, 512)),
        "bFC": f32(np.asarray(L0_ffn_b_fc).reshape(32, 128).T),
        "aFCT": f32(np.asarray(L0_ffn_A_fc).T, ), "bFCbT": f32(np.asarray(L0_ffn_B_fc).T),
        "wPRr": f32(np.asarray(L0_ffn_W_proj).T.reshape(4, 8, 128, 16, 64)
                    .transpose(0, 3, 2, 1, 4).reshape(8192, 512)),
        "bPR": f32(np.asarray(L0_ffn_b_proj).reshape(8, 128).T),
        "aPRT": f32(np.asarray(L0_ffn_A_proj).T),
        "bPRbT": f32(np.asarray(L0_ffn_B_proj).T),
        "ropeC": ropeC, "ropeD": ropeD, "pswap": psw, "trim": tri,
        "ident": idm, "sel2": sel,
    }
    x = np.asarray(L0_attn_x, dtype=np.float32)
    in_maps = []
    for b in range(8):
        m = dict(shared)
        m["xT"] = np.ascontiguousarray(x[b].T)
        in_maps.append(m)
    res = run_bass_kernel_spmd(nc, in_maps, core_ids=list(range(8)))
    outs = [np.ascontiguousarray(np.asarray(r["out"], dtype=np.float32).T)
            for r in res.results]
    return np.stack(outs, axis=0)
